# version 16
# baseline (speedup 1.0000x reference)
import numpy as np
import ml_dtypes
from contextlib import ExitStack

import concourse.bass as bass
import concourse.mybir as mybir

F32 = mybir.dt.float32
BF16 = mybir.dt.bfloat16
AF = mybir.ActivationFunctionType
ALU = mybir.AluOpType
AX = mybir.AxisListType


class CFG:
    def __init__(self, B=2, T=2048, D=1024, H=2048):
        self.B, self.T, self.D, self.H = B, T, D, H
        self.E = 16
        self.N = 16
        self.TC = 512
        assert T % self.TC == 0 and D % 128 == 0 and H % 128 == 0
        self.tc_n = T // self.TC
        self.dc_n = D // 128
        self.hc_n = H // 128
        self.CAP = 512 if T == 2048 else T // 4 + 128
        self.spc = self.CAP // 128
        self.hcg = min(2, self.hc_n)
        assert self.hc_n % self.hcg == 0
        self.hcg_n = self.hc_n // self.hcg
        self.tb_n = T // 128
        self.dhalf_n = max(1, D // 512)
        self.dhw = min(512, D)


class Sem:
    __slots__ = ("h", "n", "name")

    def __init__(self, ctx, nc, name):
        self.h = ctx.enter_context(nc.semaphore(name))
        self.n = 0
        self.name = name

    def inc(self, instr, k=1):
        instr.then_inc(self.h, k)
        self.n += k
        return self.n

    def tok(self):
        return (self, self.n)


def wait_tok(eng, tok):
    if tok is not None:
        eng.wait_ge(tok[0].h, tok[1])


def build(nc, cfg: CFG):
    B, T, D, H, E, N = cfg.B, cfg.T, cfg.D, cfg.H, cfg.E, cfg.N
    TC, tc_n, dc_n, hc_n, hcg, hcg_n, tb_n = (
        cfg.TC, cfg.tc_n, cfg.dc_n, cfg.hc_n, cfg.hcg, cfg.hcg_n, cfg.tb_n)
    tg_n = tc_n
    tpc = TC // 128
    dhalf_n, dhw = cfg.dhalf_n, cfg.dhw
    CAP, spc = cfg.CAP, cfg.spc
    from concourse.bass import IndirectOffsetOnAxis

    x_h = nc.dram_tensor("L0_attn_sequence", [B, T, D], F32, kind="ExternalInput").ap()
    A_h = nc.dram_tensor("L0_attn_A", [D, N], F32, kind="ExternalInput").ap()
    WB_h = nc.dram_tensor("L0_attn_W_B", [N, D], F32, kind="ExternalInput").ap()
    bB_h = nc.dram_tensor("L0_attn_b_B", [N, 1], F32, kind="ExternalInput").ap()
    WC_h = nc.dram_tensor("L0_attn_W_C", [N, D], F32, kind="ExternalInput").ap()
    bC_h = nc.dram_tensor("L0_attn_b_C", [N, 1], F32, kind="ExternalInput").ap()
    WD_h = nc.dram_tensor("L0_attn_W_Delta", [1, D], F32, kind="ExternalInput").ap()
    bD_h = nc.dram_tensor("L0_attn_b_Delta", [1, 1], F32, kind="ExternalInput").ap()
    biasD_h = nc.dram_tensor("L0_attn_bias_Delta", [1, 1], F32, kind="ExternalInput").ap()
    gate_h = nc.dram_tensor("L0_ffn_gate", [D, E], F32, kind="ExternalInput").ap()
    W1_h = nc.dram_tensor("L0_ffn_W1", [E, D, H], F32, kind="ExternalInput").ap()
    b1_h = nc.dram_tensor("L0_ffn_b1", [E, H], F32, kind="ExternalInput").ap()
    W2_h = nc.dram_tensor("L0_ffn_W2", [E, H, D], F32, kind="ExternalInput").ap()
    b2_h = nc.dram_tensor("L0_ffn_b2", [E, D], F32, kind="ExternalInput").ap()
    idn32_h = nc.dram_tensor("aux_idn32", [128, 128], F32, kind="ExternalInput").ap()
    idnb_h = nc.dram_tensor("aux_idnb", [128, 128], BF16, kind="ExternalInput").ap()
    onesc_h = nc.dram_tensor("aux_onesc", [128, 1], F32, kind="ExternalInput").ap()
    onescb_h = nc.dram_tensor("aux_onescb", [128, 1], BF16, kind="ExternalInput").ap()
    fconst_h = nc.dram_tensor("aux_fconst", [128, 8], F32, kind="ExternalInput").ap()
    ltri_h = nc.dram_tensor("aux_ltri", [128, 128], F32, kind="ExternalInput").ap()
    lt_h = nc.dram_tensor("aux_lt", [cfg.tb_n + 1, cfg.tb_n], F32, kind="ExternalInput").ap()
    ecap_h = nc.dram_tensor("aux_ecap", [1, E], F32, kind="ExternalInput").ap()
    onesE_h = nc.dram_tensor("aux_onesE", [128, E], F32, kind="ExternalInput").ap()
    zrow_h = nc.dram_tensor("aux_zrow", [128, D], BF16, kind="ExternalInput").ap()

    out_h = nc.dram_tensor("out", [B, T, D], F32, kind="ExternalOutput").ap()

    w1s_h = nc.dram_tensor("w1s", [E, hcg_n, 128, hcg * dc_n * 128], BF16, kind="Internal").ap()
    w2s_h = nc.dram_tensor("w2s", [E, hcg_n, 128, hcg * D], BF16, kind="Internal").ap()
    x1d_h = nc.dram_tensor("x1d", [B, dc_n, 128, T], BF16, kind="Internal").ap()
    rows_h = nc.dram_tensor("rowsb", [2 * N + 1, TC], F32, kind="Internal").ap()
    rows16_h = nc.dram_tensor("rowsb16", [2 * N, TC], BF16, kind="Internal").ap()
    lnr_h = nc.dram_tensor("lnrb", [2, TC], F32, kind="Internal").ap()
    xg_h = nc.dram_tensor("xg", [E * CAP + 1, D], BF16, kind="Internal").ap()
    og_h = nc.dram_tensor("og", [E * CAP + 1, D], BF16, kind="Internal").ap()
    cntd_h = nc.dram_tensor("cntd", [tb_n, E], F32, kind="Internal").ap()
    based_h = nc.dram_tensor("based", [tb_n, E], F32, kind="Internal").ap()

    ctx = ExitStack()
    e_ = ctx.enter_context

    idn32 = e_(nc.sbuf_tensor("idn32", [128, 128], F32))
    idnb = e_(nc.sbuf_tensor("idnb", [128, 128], BF16))
    onesc = e_(nc.sbuf_tensor("onesc", [128, 1], F32))
    onescb = e_(nc.sbuf_tensor("onescb", [128, 1], BF16))
    A_sb = e_(nc.sbuf_tensor("A_sb", [128, dc_n * N], F32))
    invA_sb = e_(nc.sbuf_tensor("invA_sb", [128, dc_n * N], F32))
    gate_sb = e_(nc.sbuf_tensor("gate_sb", [128, dc_n * E], F32))
    wbcd_sb = e_(nc.sbuf_tensor("wbcd_sb", [128, dc_n * 33], F32))
    bias33 = e_(nc.sbuf_tensor("bias33", [33, 1], F32))
    biasd_tmp = e_(nc.sbuf_tensor("biasd_tmp", [33, 1], F32))
    b1_sb = e_(nc.sbuf_tensor("b1_sb", [128, hc_n * E], F32))
    b2_sb = e_(nc.sbuf_tensor("b2_sb", [E, D], BF16))
    fc_sb = e_(nc.sbuf_tensor("fc_sb", [128, 8], F32))
    ltri_sb = e_(nc.sbuf_tensor("ltri_sb", [128, 128], F32))
    lt_sb = e_(nc.sbuf_tensor("lt_sb", [tb_n + 1, tb_n], F32))
    onesE_sb = e_(nc.sbuf_tensor("onesE_sb", [128, E], F32))
    cntr_sb = e_(nc.sbuf_tensor("cntr_sb", [tb_n + 1, E], F32))
    nc.const_aps.register(F32, 1e-5, fc_sb[:, 0:1])
    nc.const_aps.register(F32, -0.5, fc_sb[:, 1:2])
    nc.const_aps.register(F32, 1.0 / D, fc_sb[:, 2:3])

    PSF = e_(nc.psum_tensor("PSF", [128, 3584], F32))
    PSB = e_(nc.psum_tensor("PSB", [128, 1024], BF16))

    def ps1(slot, w=None):
        w = TC if w is None else w
        return PSF[:, 2048 + slot * 512:2048 + slot * 512 + w]

    def stgA(p0, p1, c0, c1):
        return PSF[p0:p1, 3072 + c0:3072 + c1]

    def stgB(p0, p1, c0, c1):
        return PSF[p0:p1, 2560 + c0:2560 + c1]

    def S(name):
        return Sem(ctx, nc, name)

    s_ld = S("ld"); s_ldg = S("ldg"); s_tab = S("tab")
    s_p0l = S("p0l"); s_p0c = S("p0c")
    s_p0w = [S("p0wa"), S("p0wb")]; s_p0w2 = S("p0w2")
    s_xn = [S("xn0"), S("xn1")]
    s_stgAr = S("stgAr"); s_stgAf = S("stgAf")
    s_stgBr = S("stgBr"); s_stgBf = S("stgBf")
    s_bcd = S("bcd"); s_row = S("row")
    s_dd = S("dd"); s_bm = [S("bm0"), S("bm1")]; s_cm = [S("cm0"), S("cm1")]; s_lnw = S("lnw")
    s_row16 = S("row16")
    s_ab = S("ab"); s_bx = S("bx"); s_in1 = S("in1"); s_scan = S("scan"); s_icm = S("icm")
    s_carry = S("carry"); s_yt = S("yt"); s_yadd = S("yadd"); s_yinit = S("yinit")
    s_sq = S("sq"); s_sqmm = S("sqmm"); s_lnr = S("lnr")
    s_x1n = S("x1n"); s_x1 = S("x1"); s_x1d = [S("x1d0"), S("x1d1")]; s_x1l = [S("x1l0"), S("x1l1")]
    s_gv = S("gv"); s_gs = S("gs"); s_g = S("g")
    s_w1 = [S("w1a"), S("w1b")]; s_w2 = S("w2")
    s_ps1r = S("ps1r"); s_hs = S("hs")
    s_ps2r = S("ps2r"); s_accu = S("accu"); s_acci = S("acci")
    s_psbr = S("psbr"); s_psbf = S("psbf")
    s_ln2v = S("ln2v"); s_ln2s = S("ln2s"); s_outv = S("outv")
    s_vch = S("vch"); s_sch = S("sch"); s_gch = S("gch")
    s_st = [S("st0"), S("st1")]
    s_rte = S("rte"); s_cb = S("cb"); s_bas = [S("bas0"), S("bas1")]
    s_xsc = S("xsc"); s_xgl = S("xgl"); s_zf = S("zf")
    s_ogc = S("ogc"); s_ogw = [S("ogw0"), S("ogw1")]
    s_og3 = [S("og30"), S("og31")]

    SY, SC, VE, PE, GP = nc.sync, nc.scalar, nc.vector, nc.tensor, nc.gpsimd

    blk = e_(nc.Block())

    def dma(eng, out, in_, sem):
        eng.dma_start(out=out, in_=in_).then_inc(sem.h, 16)
        sem.n += 16
        return sem.n

    stg_free = {"A": None, "B": None}

    def stage(bank, m_rows, m_cols, pe_emit, ev_eng, dst, ev_sem):
        reg, rs = (stgA, s_stgAr) if bank == "A" else (stgB, s_stgBr)
        wait_tok(PE, stg_free[bank])
        rs.inc(pe_emit())
        ev_eng.wait_ge(rs.h, rs.n)
        ist = ev_eng.copy(out=dst, in_=reg(0, m_rows, 0, m_cols))
        ev_sem.inc(ist)
        stg_free[bank] = ev_sem.tok()
        return ev_sem.tok()

    tabctx = ExitStack()
    wbt = tabctx.enter_context(nc.sbuf_tensor("wbt", [N, D], F32))
    wct = tabctx.enter_context(nc.sbuf_tensor("wct", [N, D], F32))
    wdt = tabctx.enter_context(nc.sbuf_tensor("wdt", [1, D], F32))
    b1t = tabctx.enter_context(nc.sbuf_tensor("b1t", [E, H], F32))
    dma(SY, idn32[:], idn32_h, s_ld)
    dma(SY, idnb[:], idnb_h, s_ld)
    dma(SY, onesc[:], onesc_h, s_ld)
    dma(SY, onescb[:], onescb_h, s_ld)
    dma(SY, fc_sb[:], fconst_h, s_ld)
    for dc in range(dc_n):
        dma(SY, A_sb[:, dc * N:(dc + 1) * N], A_h[dc * 128:(dc + 1) * 128, :], s_ld)
        dma(SY, gate_sb[:, dc * E:(dc + 1) * E], gate_h[dc * 128:(dc + 1) * 128, :], s_ld)
    dma(SY, wbt[:], WB_h, s_ld)
    dma(SY, wct[:], WC_h, s_ld)
    dma(SY, wdt[:], WD_h, s_ld)
    dma(SY, b1t[:], b1_h, s_ld)
    dma(GP, b2_sb[:], b2_h, s_ldg)
    dma(SY, ltri_sb[:], ltri_h, s_ld)
    dma(SY, lt_sb[:], lt_h, s_ld)
    dma(SY, onesE_sb[:], onesE_h, s_ld)
    dma(SY, cntr_sb[tb_n:tb_n + 1, :], ecap_h, s_ld)
    for i in range(E * CAP // 128):
        dma(GP, xg_h[i * 128:(i + 1) * 128, :], zrow_h, s_zf)
    dma(GP, og_h[E * CAP:E * CAP + 1, :], zrow_h[0:1, :], s_zf)
    ZF_ALL = s_zf.n
    dma(SY, bias33[0:N, 0:1], bB_h, s_ld)
    dma(SY, bias33[N:2 * N, 0:1], bC_h, s_ld)
    dma(SY, bias33[32:33, 0:1], bD_h, s_ld)
    dma(SY, biasd_tmp[32:33, 0:1], biasD_h, s_ld)
    LD_ALL = s_ld.n

    VE.wait_ge(s_ld.h, LD_ALL)
    VE.reciprocal(out=invA_sb[:], in_=A_sb[:])
    s_tab.inc(VE.tensor_tensor(out=bias33[32:33, 0:1], in0=bias33[32:33, 0:1],
                               in1=biasd_tmp[32:33, 0:1], op=ALU.add))

    PE.wait_ge(s_ld.h, LD_ALL)
    tab_jobs = []
    for dc in range(dc_n):
        slc = slice(dc * 128, (dc + 1) * 128)
        tab_jobs.append((wbt[:, slc], N, wbcd_sb[:, dc * 33:dc * 33 + N]))
        tab_jobs.append((wct[:, slc], N, wbcd_sb[:, dc * 33 + N:dc * 33 + 2 * N]))
        tab_jobs.append((wdt[:, slc], 1, wbcd_sb[:, dc * 33 + 32:dc * 33 + 33]))
    for hc in range(hc_n):
        tab_jobs.append((b1t[:, hc * 128:(hc + 1) * 128], E, b1_sb[:, hc * E:(hc + 1) * E]))
    for src, m, dst in tab_jobs:
        kk = src.shape[0]
        stage("A", 128, m,
              lambda src=src, m=m, kk=kk: PE.transpose(out=stgA(0, 128, 0, m), in_=src,
                                                       identity=idn32[0:kk, 0:kk]),
              SC, dst, s_stgAf)
    s_tab.inc(SC.copy(out=biasd_tmp[0:1, 0:1], in_=bias33[0:1, 0:1]))
    TAB_ALL = s_tab.n
    tabctx.close()

    def p0_gen():
        for e in range(E):
            for g in range(hcg_n):
                src = W2_h[e, g * hcg * 128:(g + 1) * hcg * 128, :]
                src = src.rearrange("(hcl p) j -> p hcl j", p=128)
                dst = w2s_h[e, g].rearrange("p (hcl j) -> p hcl j", hcl=hcg)
                dma(GP, dst, src, s_p0w2)
            yield
        w1tmp = [w2sb[:, dc * H:(dc + 1) * H] for dc in range(dc_n)]
        asmw = hcg * dc_n * 128
        asm1 = [hs[:, i * asmw:(i + 1) * asmw] for i in range(2)]
        cpe = hc_n * dc_n
        GP.wait_ge(s_tab.h, TAB_ALL)
        for e in range(E):
            for dc in range(dc_n):
                if e > 0:
                    GP.wait_ge(s_p0c.h, e * cpe)
                dma(GP, w1tmp[dc], W1_h[e, dc * 128:(dc + 1) * 128, :], s_p0l)
            yield
            for g in range(hcg_n):
                gi = e * hcg_n + g
                a = asm1[gi % 2]
                VE.wait_ge(s_p0l.h, s_p0l.n)
                if gi >= 2:
                    VE.wait_ge(s_p0w[gi % 2].h, s_p0w[gi % 2].n)
                last = None
                for hcl in range(hcg):
                    hc = g * hcg + hcl
                    for dc in range(dc_n):
                        o = hcl * dc_n * 128 + dc * 128
                        last = VE.tensor_copy(out=a[:, o:o + 128],
                                              in_=w1tmp[dc][:, hc * 128:(hc + 1) * 128])
                s_p0c.inc(last, hcg * dc_n)
                SY.wait_ge(s_p0c.h, s_p0c.n)
                dma(SY, w1s_h[e, g], a, s_p0w[gi % 2])
                if g % 2 == 1:
                    yield

    xn = [e_(nc.sbuf_tensor(f"xn{i}", [128, D], F32)) for i in range(1)]
    xc = [e_(nc.sbuf_tensor(f"xc{i}", [128, TC], F32)) for i in range(dc_n)]
    y_sb = [e_(nc.sbuf_tensor(f"y{i}", [128, TC], F32)) for i in range(dc_n)]
    bcd_sb = e_(nc.sbuf_tensor("bcd_sb", [33, TC], F32))
    sp_tmp = e_(nc.sbuf_tensor("sp_tmp", [33, TC], F32))
    drep = e_(nc.sbuf_tensor("drep", [128, TC], F32))
    bmrep = [e_(nc.sbuf_tensor(f"bmrep{i}", [128, TC], BF16)) for i in range(2)]
    cmrep = [e_(nc.sbuf_tensor(f"cmrep{i}", [128, TC], BF16)) for i in range(2)]
    icm_b = [e_(nc.sbuf_tensor(f"icmb{i}", [128, TC], BF16)) for i in range(2)]
    rrep32 = e_(nc.sbuf_tensor("rrep32", [128, TC], F32))
    ln1tmp = e_(nc.sbuf_tensor("ln1tmp", [128, TC], F32))
    ab_b = [e_(nc.sbuf_tensor(f"abb{i}", [128, TC], BF16)) for i in range(6)]
    bx_b = [e_(nc.sbuf_tensor(f"bxb{i}", [128, TC], BF16)) for i in range(4)]
    in1_b = [e_(nc.sbuf_tensor(f"in1b{i}", [128, TC], BF16)) for i in range(2)]
    sc_b = [e_(nc.sbuf_tensor(f"scb{i}", [128, TC], F32)) for i in range(2)]
    yt_b = [e_(nc.sbuf_tensor(f"ytb{i}", [128, TC], BF16)) for i in range(4)]
    sq_b = in1_b
    carry = e_(nc.sbuf_tensor("carry", [128, dc_n * N], F32))
    lnrow = e_(nc.sbuf_tensor("lnrow", [1, 2 * TC], F32))
    eg_sb = e_(nc.sbuf_tensor("eg_sb", [128, E], F32))
    m8_sb = e_(nc.sbuf_tensor("m8_sb", [128, 8], F32))
    mask_sb = e_(nc.sbuf_tensor("mask_sb", [128, E], F32))
    gsel_sb = e_(nc.sbuf_tensor("gsel_sb", [128, E], F32))
    cols = e_(nc.sbuf_tensor("cols", [128, 16], F32))
    g_sb = [e_(nc.sbuf_tensor(f"g_sb{i}", [128, tb_n * E], F32)) for i in range(2)]
    gT_sb = [e_(nc.sbuf_tensor(f"gT_sb{i}", [E, T], BF16)) for i in range(2)]
    x1stg = [e_(nc.sbuf_tensor(f"x1stg{i}", [128, TC], BF16)) for i in range(1)]
    x1tg = [e_(nc.sbuf_tensor(f"x1tg{i}", [128, dc_n * TC], BF16)) for i in range(2)]
    acc = e_(nc.sbuf_tensor("acc", [128, tb_n * D], BF16))
    hs = e_(nc.sbuf_tensor("hs", [128, max(hc_n * TC, 7 * D)], BF16))
    mskt_sb = e_(nc.sbuf_tensor("mskt_sb", [128, E], F32))
    srank_sb = e_(nc.sbuf_tensor("srank_sb", [128, E], F32))
    cnt_sb = e_(nc.sbuf_tensor("cnt_sb", [1, tb_n * E], F32))
    bs_sb = e_(nc.sbuf_tensor("bs_sb", [tb_n, E], F32))
    basebc = e_(nc.sbuf_tensor("basebc", [128, 2 * E], F32))
    rt_sb = e_(nc.sbuf_tensor("rt_sb", [128, 5 * E + 3], F32))
    ptr_u = e_(nc.sbuf_tensor("ptr_u", [128, 2 * tb_n * 3], mybir.dt.uint32))
    gval_sb = e_(nc.sbuf_tensor("gval_sb", [128, 2 * tb_n * 3], F32))
    ogst = e_(nc.sbuf_tensor("ogst", [128, D], BF16))
    w1buf = [e_(nc.sbuf_tensor(f"w1buf{i}", [128, hcg * dc_n * 128], BF16)) for i in range(2)]
    w2sb = e_(nc.sbuf_tensor("w2sb", [128, hc_n * D], BF16))
    out_sb = [e_(nc.sbuf_tensor(f"out_sb{i}", [128, D], F32)) for i in range(1)]

    c_rowmax = cols[:, 0:1]; c_negmax = cols[:, 1:2]
    c_den = cols[:, 2:3]; c_den2 = cols[:, 3:4]; c_rec = cols[:, 4:5]
    c_sum = cols[:, 5:6]; c_sq = cols[:, 6:7]; c_mean = cols[:, 7:8]
    c_m2 = cols[:, 8:9]; c_var = cols[:, 9:10]; c_lnv = cols[:, 10:11]; c_rstd = cols[:, 11:12]

    for ENG in (PE, VE, SC, GP, SY):
        ENG.wait_ge(s_tab.h, TAB_ALL)
        ENG.wait_ge(s_ldg.h, 16)

    carry_val = {}
    acc_last = {}
    w1_load_tok = []
    w1_free_tok = []
    w1_plan = []
    w1_state = {"emitted": 0}

    def pump_w1():
        while w1_state["emitted"] < len(w1_plan):
            k = w1_state["emitted"]
            if k >= 2 and w1_free_tok[k - 2] is None:
                return
            if k >= 2:
                wait_tok(SY, w1_free_tok[k - 2])
            e_, g_ = w1_plan[k]
            dma(SY, w1buf[k % 2][:], w1s_h[e_, g_], s_w1[k % 2])
            w1_load_tok.append((s_w1[k % 2], s_w1[k % 2].n))
            w1_state["emitted"] = k + 1
    ps1_free_at = []
    hs_free_tok = []
    xn_free_tok = []
    rep = {}
    gpe_token = None

    shared = {'gpe': None}
    mstate = {}
    ffn_state = {"x1tg_free": [None, None], "ps2_free": [None, None],
                 "ogw_tok": [None, None], "xgl_all": None, "fadd_last": None}
    def mamba_gen(b):
        gp_ = b % 2
        gpe_token = shared['gpe']
        for tc in range(tc_n):
            t0 = tc * TC
            for tt in range(tpc):
                kx = s_xn[0].n // 16
                if kx >= 1:
                    wait_tok(SY, xn_free_tok[kx - 1])
                slx = 0
                dma(SY, xn[slx][:], x_h[b, t0 + tt * 128:t0 + (tt + 1) * 128, :], s_xn[slx])
                PE.wait_ge(s_xn[slx].h, s_xn[slx].n)
                nuse = (dc_n + 3) // 4
                for u in range(nuse):
                    ndc = min(4, dc_n - u * 4)
                    bank = "A"
                    reg, rs = stgA, s_stgAr
                    fsem = s_stgAf
                    wait_tok(PE, stg_free[bank])
                    last = None
                    for i in range(ndc):
                        dc = u * 4 + i
                        last = PE.transpose(out=reg(0, 128, i * 128, (i + 1) * 128),
                                            in_=xn[slx][:, dc * 128:(dc + 1) * 128],
                                            identity=idn32[:])
                    rs.inc(last)
                    SC.wait_ge(rs.h, rs.n)
                    last2 = None
                    for i in range(ndc):
                        dc = u * 4 + i
                        last2 = SC.copy(out=xc[dc][:, tt * 128:(tt + 1) * 128],
                                        in_=reg(0, 128, i * 128, (i + 1) * 128))
                    fsem.inc(last2)
                    stg_free[bank] = fsem.tok()
                xn_free_tok.append(stg_free["A"])
                yield
            xc_tok = xn_free_tok[-1]

            wait_tok(PE, xc_tok)
            wait_tok(PE, stg_free["A"])
            last = None
            for dc in range(dc_n):
                last = PE.matmul(out=stgA(0, 33, 0, TC), lhsT=wbcd_sb[:, dc * 33:(dc + 1) * 33],
                                 rhs=xc[dc][:, 0:TC], start=(dc == 0), stop=(dc == dc_n - 1))
            s_stgAr.inc(last)
            VE.wait_ge(s_stgAr.h, s_stgAr.n)
            ist = VE.tensor_scalar(out=bcd_sb[:], in0=stgA(0, 33, 0, TC), scalar1=bias33[:],
                                   scalar2=None, op0=ALU.add)
            s_stgAf.inc(ist)
            stg_free["A"] = s_stgAf.tok()
            bcd_tok = s_stgAf.tok()
            z = bcd_sb[32:33, :]
            t1 = sp_tmp[32:33, 0:TC]
            wait_tok(SC, bcd_tok)
            s_sch.inc(SC.activation(out=t1, in_=z, func=AF.Abs))
            SC.wait_ge(s_sch.h, s_sch.n)
            s_sch.inc(SC.activation(out=t1, in_=t1, func=AF.Exp, scale=-1.0))
            SC.wait_ge(s_sch.h, s_sch.n)
            s_sch.inc(SC.activation(out=t1, in_=t1, func=AF.Ln, bias=1.0))
            SC.wait_ge(s_sch.h, s_sch.n)
            s_bcd.inc(SC.activation(out=z, in_=z, func=AF.Relu))
            VE.wait_ge(s_bcd.h, s_bcd.n)
            s_bcd.inc(VE.tensor_tensor(out=z, in0=z, in1=t1, op=ALU.add))
            SY.wait_ge(s_bcd.h, s_bcd.n)
            dma(SY, rows_h, bcd_sb[:], s_row)
            GP.wait_ge(s_bcd.h, s_bcd.n)
            dma(GP, rows16_h, bcd_sb[0:2 * N, :], s_row16)
            ROW_V = s_row.n
            ROW16_V = s_row16.n
            SY.wait_ge(s_row.h, ROW_V)
            SY.wait_ge(s_ab.h, s_ab.n)
            SY.wait_ge(s_x1n.h, s_x1n.n)
            dma(SY, drep[:], rows_h[32:33, :].to_broadcast((128, TC)), s_dd)
            DD_V = s_dd.n
            yield

            wait_tok(GP, xc_tok)
            if b > 0 or tc > 0:
                wait_tok(GP, gpe_token)
                GP.wait_ge(s_x1.h, s_x1.n)
            last = None
            for dc in range(dc_n):
                last = GP.tensor_copy(out=y_sb[dc][:], in_=xc[dc][:])
            s_yinit.inc(last)
            YINIT_V = s_yinit.n

            ylast = {db: None for db in range(dc_n)}
            iters = N * dc_n
            BMv = {}
            CMv = {}

            def emit_repdma(n):
                sl = n % 2
                SY.wait_ge(s_row16.h, ROW16_V)
                SY.wait_ge(s_row.h, ROW_V)
                if n >= 2:
                    SY.wait_ge(s_bx.h, BX_BASE0 + (n - 1) * dc_n)
                    SY.wait_ge(s_icm.h, ICM_BASE0 + (n - 1) * dc_n)
                dma(SY, bmrep[sl][:], rows16_h[n:n + 1, :].to_broadcast((128, TC)), s_bm[sl])
                BMv[n] = s_bm[sl].n
                dma(SY, cmrep[sl][:], rows16_h[N + n:N + n + 1, :].to_broadcast((128, TC)), s_cm[sl])
                CMv[n] = s_cm[sl].n

            BX_BASE0 = s_bx.n
            YT_BASE0 = s_yt.n
            SCAN_BASE0 = s_scan.n
            IN1_BASE0 = s_in1.n
            AB_BASE0 = s_ab.n
            YADD_BASE0 = s_yadd.n
            ICM_BASE0 = s_icm.n

            def fix_rep_wait(n):
                pass

            emit_repdma(0)
            if N > 1:
                emit_repdma(1)
            for k in range(iters + 2):
                if k < iters:
                    n, db = k // dc_n, k % dc_n
                    sl4 = k % 4
                    sl6 = k % 6
                    ci = db * N + n
                    SC.wait_ge(s_dd.h, DD_V)
                    if k >= 6:
                        SC.wait_ge(s_scan.h, SCAN_BASE0 + k - 5)
                    s_ab.inc(SC.activation(out=ab_b[sl6][:], in_=drep[:], func=AF.Exp,
                                           scale=A_sb[:, ci:ci + 1]))
                    SC.wait_ge(s_cm[n % 2].h, CMv[n])
                    if k >= 2:
                        SC.wait_ge(s_yt.h, YT_BASE0 + k - 1)
                    s_icm.inc(SC.activation(out=icm_b[k % 2][:], in_=cmrep[n % 2][:],
                                            func=AF.Copy, scale=invA_sb[:, ci:ci + 1]))
                    GP.wait_ge(s_bm[n % 2].h, BMv[n])
                    if k >= 4:
                        GP.wait_ge(s_in1.h, IN1_BASE0 + k - 3)
                    s_bx.inc(GP.tensor_tensor(out=bx_b[sl4][:], in0=xc[db][:],
                                              in1=bmrep[n % 2][:], op=ALU.mult))
                if k >= 1 and k - 1 < iters:
                    j = k - 1
                    n, db = j // dc_n, j % dc_n
                    sl = j % 2
                    sl4 = j % 4
                    sl6 = j % 6
                    ci = db * N + n
                    VE.wait_ge(s_ab.h, AB_BASE0 + j + 1)
                    VE.wait_ge(s_bx.h, BX_BASE0 + j + 1)
                    s_in1.inc(VE.scalar_tensor_tensor(out=in1_b[sl][:], in0=ab_b[sl6][:], scalar=1.0,
                                                      in1=bx_b[sl4][:], op0=ALU.subtract, op1=ALU.mult))
                    VE.wait_ge(s_in1.h, s_in1.n)
                    if j >= 2:
                        VE.wait_ge(s_yt.h, YT_BASE0 + j - 1)
                    if tc == 0:
                        init = 0.0
                    else:
                        VE.wait_ge(s_carry.h, carry_val[ci])
                        init = carry[:, ci:ci + 1]
                    s_scan.inc(VE.tensor_tensor_scan(out=sc_b[sl][:], data0=ab_b[sl6][:],
                                                     data1=in1_b[sl][:], initial=init,
                                                     op0=ALU.mult, op1=ALU.add))
                    VE.wait_ge(s_scan.h, s_scan.n)
                    if tc < tc_n - 1:
                        carry_val[ci] = s_carry.inc(
                            VE.tensor_copy(out=carry[:, ci:ci + 1], in_=sc_b[sl][:, TC - 1:TC]))
                    VE.wait_ge(s_icm.h, ICM_BASE0 + j + 1)
                    if j >= 4:
                        VE.wait_ge(s_yadd.h, YADD_BASE0 + j - 3)
                    s_yt.inc(VE.tensor_tensor(out=yt_b[sl4][:], in0=sc_b[sl][:],
                                              in1=icm_b[j % 2][:], op=ALU.mult))
                if k >= 2:
                    j = k - 2
                    n, db = j // dc_n, j % dc_n
                    sl4 = j % 4
                    GP.wait_ge(s_yt.h, YT_BASE0 + j + 1)
                    GP.wait_ge(s_yinit.h, YINIT_V)
                    if ylast[db] is not None:
                        GP.wait_ge(s_yadd.h, ylast[db])
                    ylast[db] = s_yadd.inc(GP.tensor_tensor(out=y_sb[db][:], in0=y_sb[db][:],
                                                            in1=yt_b[sl4][:], op=ALU.add))
                if k < iters:
                    n, db = k // dc_n, k % dc_n
                    if db == 0 and 1 <= n and n + 1 < N:
                        emit_repdma(n + 1)
                yield
            YADD_V = s_yadd.n

            PE.wait_ge(s_yadd.h, YADD_V)
            wait_tok(PE, stg_free["A"])
            last = None
            for db in range(dc_n):
                last = PE.matmul(out=stgA(0, 1, 0, TC), lhsT=onesc[:], rhs=y_sb[db][:],
                                 start=(db == 0), stop=(db == dc_n - 1))
            s_stgAr.inc(last)
            SC.wait_ge(s_stgAr.h, s_stgAr.n)
            ist = SC.activation(out=lnrow[0:1, 0:TC], in_=stgA(0, 1, 0, TC), func=AF.Copy, scale=1.0 / D)
            s_stgAf.inc(ist)
            stg_free["A"] = s_stgAf.tok()
            mean_tok = s_stgAf.tok()
            SQMM_BASE = s_sqmm.n
            wait_tok(PE, stg_free["A"])
            for db in range(dc_n):
                VE.wait_ge(s_yadd.h, YADD_V)
                if db >= 1:
                    VE.wait_ge(s_sqmm.h, SQMM_BASE + db)
                s_sq.inc(VE.tensor_tensor(out=sq_b[0][:], in0=y_sb[db][:],
                                          in1=y_sb[db][:], op=ALU.mult))
                PE.wait_ge(s_sq.h, s_sq.n)
                mm = PE.matmul(out=stgA(0, 1, 0, TC), lhsT=onescb[:], rhs=sq_b[0][:],
                               start=(db == 0), stop=(db == dc_n - 1))
                if db == dc_n - 1:
                    s_stgAr.inc(mm)
                else:
                    s_sqmm.inc(mm)
            SC.wait_ge(s_stgAr.h, s_stgAr.n)
            ist = SC.activation(out=lnrow[0:1, TC:2 * TC], in_=stgA(0, 1, 0, TC), func=AF.Copy, scale=1.0 / D)
            s_stgAf.inc(ist)
            stg_free["A"] = s_stgAf.tok()
            msq_tok = s_stgAf.tok()
            wait_tok(SY, mean_tok)
            wait_tok(SY, msq_tok)
            dma(SY, lnr_h.rearrange("r n -> (r n)")[None, :], lnrow[0:1, :], s_lnw)
            SY.wait_ge(s_lnw.h, s_lnw.n)
            SY.wait_ge(s_ab.h, s_ab.n)
            SY.wait_ge(s_x1n.h, s_x1n.n)
            dma(SY, drep[:], lnr_h[0:1, :].to_broadcast((128, TC)), s_bm[0])
            BM_V = s_bm[0].n
            dma(SY, rrep32[:], lnr_h[1:2, :].to_broadcast((128, TC)), s_cm[0])
            CM_V = s_cm[0].n
            VE.wait_ge(s_bm[0].h, BM_V)
            VE.wait_ge(s_cm[0].h, CM_V)
            s_vch.inc(VE.tensor_tensor(out=ln1tmp[:], in0=drep[:], in1=drep[:], op=ALU.mult))
            VE.wait_ge(s_vch.h, s_vch.n)
            s_lnr.inc(VE.tensor_tensor(out=rrep32[:], in0=rrep32[:], in1=ln1tmp[:], op=ALU.subtract))
            SC.wait_ge(s_lnr.h, s_lnr.n)
            s_sch.inc(SC.activation(out=rrep32[:], in_=rrep32[:], func=AF.Ln, bias=1e-5))
            SC.wait_ge(s_sch.h, s_sch.n)
            s_lnr.inc(SC.activation(out=rrep32[:], in_=rrep32[:], func=AF.Exp, scale=-0.5))
            LNR_V = s_lnr.n
            for db in range(dc_n):
                VE.wait_ge(s_lnr.h, LNR_V)
                s_vch.inc(VE.tensor_tensor(out=y_sb[db][:], in0=y_sb[db][:], in1=drep[:], op=ALU.subtract))
                VE.wait_ge(s_vch.h, s_vch.n)
                s_x1n.inc(VE.tensor_tensor(out=y_sb[db][:], in0=y_sb[db][:], in1=rrep32[:], op=ALU.mult))
            yield
            X1N_V = s_x1n.n
            for db in range(dc_n):
                SC.wait_ge(s_x1n.h, X1N_V)
                kq = s_x1.n
                if kq >= 1:
                    SC.wait_ge(s_x1d[0].h, s_x1d[0].n)
                s_x1.inc(SC.copy(out=x1stg[0][:], in_=y_sb[db][:]))
                SY.wait_ge(s_x1.h, s_x1.n)
                dma(SY, x1d_h[b, db, :, t0:t0 + TC], x1stg[0][:], s_x1d[0])
            for tb in range(tpc):
                PE.wait_ge(s_x1n.h, X1N_V)
                wait_tok(PE, stg_free["A"])
                last = None
                for dc in range(dc_n):
                    last = PE.matmul(out=stgA(0, 128, 0, E),
                                     lhsT=y_sb[dc][:, tb * 128:(tb + 1) * 128],
                                     rhs=gate_sb[:, dc * E:(dc + 1) * E],
                                     start=(dc == 0), stop=(dc == dc_n - 1))
                s_stgAr.inc(last)
                VE.wait_ge(s_stgAr.h, s_stgAr.n)
                s_gv.inc(VE.reduce_max(out=c_rowmax, in_=stgA(0, 128, 0, E), axis=AX.X))
                VE.wait_ge(s_gv.h, s_gv.n)
                s_gv.inc(VE.tensor_scalar(out=c_negmax, in0=c_rowmax, scalar1=-1.0,
                                          scalar2=None, op0=ALU.mult))
                SC.wait_ge(s_gv.h, s_gv.n)
                SC.wait_ge(s_g.h, s_g.n)
                ist = SC.activation(out=eg_sb[:], in_=stgA(0, 128, 0, E), func=AF.Exp, bias=c_negmax)
                s_gs.inc(ist)
                stg_free["A"] = s_gs.tok()
                VE.wait_ge(s_gs.h, s_gs.n)
                s_gv.inc(VE.max(out=m8_sb[:], in_=eg_sb[:]))
                VE.wait_ge(s_gv.h, s_gv.n)
                s_gv.inc(VE.tensor_scalar(out=mask_sb[:], in0=eg_sb[:], scalar1=m8_sb[:, 2:3],
                                          scalar2=None, op0=ALU.is_ge))
                VE.wait_ge(s_gv.h, s_gv.n)
                s_gv.inc(VE.tensor_tensor(out=gsel_sb[:], in0=eg_sb[:], in1=mask_sb[:], op=ALU.mult))
                VE.wait_ge(s_gv.h, s_gv.n)
                s_gv.inc(VE.reduce_sum(out=c_den, in_=gsel_sb[:], axis=AX.X))
                VE.wait_ge(s_gv.h, s_gv.n)
                s_gv.inc(VE.tensor_scalar(out=c_den2, in0=c_den, scalar1=1e-12,
                                          scalar2=None, op0=ALU.max))
                VE.wait_ge(s_gv.h, s_gv.n)
                s_gv.inc(VE.reciprocal(out=c_rec, in_=c_den2))
                VE.wait_ge(s_gv.h, s_gv.n)
                gc0 = (tc * tpc + tb) * E
                s_g.inc(VE.tensor_scalar(out=g_sb[gp_][:, gc0:gc0 + E], in0=gsel_sb[:],
                                         scalar1=c_rec, scalar2=None, op0=ALU.mult))
                PE.wait_ge(s_g.h, s_g.n)
                tcol0 = (tc * tpc + tb) * 128
                rep["gT"] = stage("A", E, 128,
                                  lambda gc0=gc0: PE.transpose(out=stgA(0, E, 0, 128),
                                                               in_=g_sb[gp_][:, gc0:gc0 + E],
                                                               identity=idn32[:]),
                                  SC, gT_sb[gp_][:, tcol0:tcol0 + 128], s_stgAf)
                yield
            gpe_token = rep["gT"]
        GT_TOK = rep["gT"]
        G_V = s_g.n
        X1D_V = (s_x1d[0].n, s_x1d[1].n)

        mstate[b] = dict(GT_TOK=GT_TOK, G_V=G_V, X1D_V=X1D_V)
        shared['gpe'] = gpe_token
        yield
    def ffn_emit(b, pump):
        gp_ = b % 2
        if b == 0:
            for ENG in (SY, SC):
                ENG.wait_ge(s_p0w[0].h, s_p0w[0].n)
                ENG.wait_ge(s_p0w[1].h, s_p0w[1].n)
                ENG.wait_ge(s_p0c.h, s_p0c.n)
                ENG.wait_ge(s_p0l.h, s_p0l.n)
                ENG.wait_ge(s_p0w2.h, s_p0w2.n)
        else:
            SC.wait_ge(s_cb.h, s_cb.n)
        GT_TOK = mstate[b]['GT_TOK']
        G_V = mstate[b]['G_V']
        X1D_V = mstate[b]['X1D_V']
        x1tg_free = ffn_state["x1tg_free"]
        ps2_free = ffn_state["ps2_free"]
        ogw_tok = ffn_state["ogw_tok"]
        pcol0 = gp_ * (tb_n * 3)
        HS_GUARD = s_hs.n
        rt0 = rt_sb[:, 0:E]; rt1 = rt_sb[:, E:2 * E]
        rt2 = rt_sb[:, 2 * E:3 * E]; rt3 = rt_sb[:, 3 * E:4 * E]
        rt4 = rt_sb[:, 4 * E:5 * E]
        pf_col = rt_sb[:, 5 * E:5 * E + 1]
        pf2_col = rt_sb[:, 5 * E + 1:5 * E + 2]
        pf3_col = rt_sb[:, 5 * E + 2:5 * E + 3]

        def load_x1tg(tg):
            slg_ = tg % 2
            SY.wait_ge(s_x1d[0].h, X1D_V[0])
            SY.wait_ge(s_x1d[1].h, X1D_V[1])
            wait_tok(SY, x1tg_free[slg_])
            for db in range(dc_n):
                dma(SY, x1tg[slg_][:, db * TC:(db + 1) * TC],
                    x1d_h[b, db, :, tg * TC:tg * TC + TC], s_x1l[slg_])
            x1l_v[slg_] = s_x1l[slg_].n

        x1l_v = [0, 0]
        accp = {}
        rnk_tok = {}
        PTR_V = {}
        load_x1tg(0)
        for tg in range(tg_n):
            slg = tg % 2
            X1L_V = x1l_v[slg]
            for ti in range(tpc):
                tile = tg * tpc + ti
                PE.wait_ge(s_x1l[slg].h, X1L_V)
                PE.wait_ge(s_psbf.h, s_psbf.n)
                last = None
                for dc in range(dc_n):
                    last = PE.transpose(out=PSB[:, dc * 128:(dc + 1) * 128],
                                        in_=x1tg[slg][:, dc * TC + ti * 128:dc * TC + (ti + 1) * 128],
                                        identity=idnb[:])
                s_psbr.inc(last)
                VE.wait_ge(s_psbr.h, s_psbr.n)
                s_psbf.inc(VE.tensor_copy(out=acc[:, tile * D:(tile + 1) * D], in_=PSB[:, 0:D]))
                accp[tile] = s_psbf.tok()
                wait_tok(VE, ffn_state.get("cnt_mm_tok"))
                s_rte.inc(VE.tensor_scalar(out=mskt_sb[:], in0=g_sb[gp_][:, tile * E:(tile + 1) * E],
                                           scalar1=0.0, scalar2=None, op0=ALU.is_gt))
                MSK_V = s_rte.n

                def cnt_mm(MSK_V=MSK_V, first=(b > 0 and tile == 0)):
                    if first:
                        PE.wait_ge(s_hs.h, HS_GUARD)
                    PE.wait_ge(s_rte.h, MSK_V)
                    return PE.matmul(out=stgB(0, 1, 0, E), lhsT=onesc[:], rhs=mskt_sb[:],
                                     start=True, stop=True)
                stage("B", 1, E, cnt_mm, SC, cnt_sb[0:1, tile * E:(tile + 1) * E], s_stgBf)
                ffn_state["cnt_mm_tok"] = (s_stgBr, s_stgBr.n)
                pump(3)
            x1tg_free[slg] = (s_psbr, s_psbr.n)
            if tg + 1 < tg_n:
                load_x1tg(tg + 1)
        CNT_V = s_stgBf.n

        SY.wait_ge(s_stgBf.h, CNT_V)
        dma(SY, cntd_h.rearrange("a e -> (a e)")[None, :], cnt_sb[0:1, :], s_cb)
        SY.wait_ge(s_cb.h, s_cb.n)
        dma(SY, cntr_sb[0:tb_n, :], cntd_h, s_cb)
        CB_V = s_cb.n

        def base_mm():
            PE.wait_ge(s_cb.h, CB_V)
            return PE.matmul(out=stgA(0, tb_n, 0, E), lhsT=lt_sb[:], rhs=cntr_sb[:],
                             start=True, stop=True)
        BS_TOK = stage("A", tb_n, E, base_mm, SC, bs_sb[:], s_stgAf)
        wait_tok(SY, BS_TOK)
        dma(SY, based_h, bs_sb[:], s_cb)
        BD_V = s_cb.n

        bas_rd = ffn_state.setdefault("bas_rd", [None, None])
        rnk_rd = ffn_state.get("rnk_rd")
        srd_tok = ffn_state.get("srd_tok")
        for tile in range(tb_n):
            par = tile % 2
            c0 = pcol0 + tile * 3
            SY.wait_ge(s_cb.h, BD_V)
            wait_tok(SY, bas_rd[par])
            dma(SY, basebc[:, par * E:(par + 1) * E],
                based_h[tile:tile + 1, :].to_broadcast((128, E)), s_bas[par])
            gt_ = g_sb[gp_][:, tile * E:(tile + 1) * E]
            wait_tok(VE, rnk_rd)
            s_rte.inc(VE.tensor_scalar(out=rt4, in0=gt_, scalar1=0.0, scalar2=None, op0=ALU.is_gt))
            MSK2_V = s_rte.n

            def rank_mm(MSK2_V=MSK2_V):
                PE.wait_ge(s_rte.h, MSK2_V)
                return PE.matmul(out=stgA(0, 128, 0, E), lhsT=ltri_sb[:], rhs=rt4,
                                 start=True, stop=True)
            wait_tok(SC, srd_tok)
            srk_tok = stage("A", 128, E, rank_mm, SC, srank_sb[:], s_stgAf)
            rnk_rd = (s_stgAr, s_stgAr.n)
            VE.wait_ge(s_bas[par].h, s_bas[par].n)
            wait_tok(VE, srk_tok)
            s_rte.inc(VE.tensor_tensor(out=rt0, in0=srank_sb[:],
                                       in1=basebc[:, par * E:(par + 1) * E], op=ALU.add))
            bas_rd[par] = s_rte.tok()
            srd_tok = s_rte.tok()
            s_rte.inc(VE.tensor_tensor_scan(out=rt1, data0=onesE_sb[:], data1=rt4,
                                            initial=0.0, op0=ALU.mult, op1=ALU.add))
            VE.wait_ge(s_rte.h, s_rte.n)
            s_rte.inc(VE.tensor_tensor(out=rt0, in0=rt0, in1=rt4, op=ALU.mult))
            for j in range(3):
                VE.wait_ge(s_rte.h, s_rte.n)
                s_rte.inc(VE.tensor_scalar(out=rt2, in0=rt1, scalar1=float(j + 1),
                                           scalar2=None, op0=ALU.is_equal))
                VE.wait_ge(s_rte.h, s_rte.n)
                s_rte.inc(VE.tensor_tensor(out=rt2, in0=rt2, in1=rt4, op=ALU.mult))
                VE.wait_ge(s_rte.h, s_rte.n)
                s_rte.inc(VE.tensor_tensor(out=rt3, in0=rt0, in1=rt2, op=ALU.mult))
                s_rte.inc(VE.reduce_sum(out=pf2_col, in_=rt2, axis=AX.X))
                VE.wait_ge(s_rte.h, s_rte.n)
                s_rte.inc(VE.reduce_sum(out=pf_col, in_=rt3, axis=AX.X))
                s_rte.inc(VE.scalar_tensor_tensor(out=pf3_col, in0=pf2_col, scalar=1.0,
                                                  in1=fc_sb[:, 3:4], op0=ALU.is_lt, op1=ALU.mult))
                VE.wait_ge(s_rte.h, s_rte.n)
                s_rte.inc(VE.tensor_tensor(out=pf_col, in0=pf_col, in1=pf3_col, op=ALU.add))
                VE.wait_ge(s_rte.h, s_rte.n)
                s_rte.inc(VE.tensor_copy(out=ptr_u[:, c0 + j:c0 + j + 1], in_=pf_col))
                s_rte.inc(VE.tensor_tensor(out=rt3, in0=gt_, in1=rt2, op=ALU.mult))
                VE.wait_ge(s_rte.h, s_rte.n)
                s_rte.inc(VE.reduce_sum(out=gval_sb[:, c0 + j:c0 + j + 1], in_=rt3, axis=AX.X))
            PTR_V[tile] = s_rte.n
            ffn_state["rnk_rd"] = rnk_rd
            ffn_state["srd_tok"] = srd_tok
            pump(3)
        GP.wait_ge(s_rte.h, PTR_V[tb_n - 1])
        wait_tok(GP, accp[tb_n - 1])
        GP.wait_ge(s_zf.h, ZF_ALL)
        if b > 0:
            wait_tok(GP, ffn_state["xgl_all"])
        for tile in range(tb_n):
            c0 = pcol0 + tile * 3
            for j in range(3):
                ist = GP.indirect_dma_start(
                    out=xg_h, out_offset=IndirectOffsetOnAxis(ptr_u[:, c0 + j:c0 + j + 1], 0),
                    in_=acc[:, tile * D:(tile + 1) * D], in_offset=None)
                s_xsc.inc(ist, 16)
        pump(6)
        XSC_ALL = s_xsc.n

        trn_tok = None
        w1ph_tok = None

        def load_xg(e_, dep_tok):
            SY.wait_ge(s_xsc.h, XSC_ALL)
            wait_tok(SY, dep_tok)
            for si in range(spc):
                dma(SY, x1tg[0][:, si * D:(si + 1) * D],
                    xg_h[e_ * CAP + si * 128:e_ * CAP + (si + 1) * 128, :], s_xgl)
            return s_xgl.n

        def t_round(si, first, xglv, war_tok):
            if first:
                wait_tok(PE, war_tok)
                PE.wait_ge(s_xgl.h, xglv)
            PE.wait_ge(s_psbf.h, s_psbf.n)
            last = None
            for dc in range(dc_n):
                last = PE.transpose(out=PSB[:, dc * 128:(dc + 1) * 128],
                                    in_=x1tg[0][:, si * D + dc * 128:si * D + (dc + 1) * 128],
                                    identity=idnb[:])
            s_psbr.inc(last)
            SC.wait_ge(s_psbr.h, s_psbr.n)
            if first:
                wait_tok(SC, war_tok)
            last = None
            for dc in range(dc_n):
                last = SC.copy(out=x1tg[1][:, dc * CAP + si * 128:dc * CAP + (si + 1) * 128],
                               in_=PSB[:, dc * 128:(dc + 1) * 128])
            s_psbf.inc(last)

        xgl_v = load_xg(0, x1tg_free[0])
        for si in range(spc):
            t_round(si, si == 0, xgl_v, x1tg_free[1])
        trn_tok = (s_psbr, s_psbr.n)
        TRNC_V = s_psbf.n
        if E > 1:
            xgl_v = load_xg(1, trn_tok)
        for g in range(hcg_n):
            w1_plan.append((0, g))
            w1_free_tok.append(None)
        for e in range(E):
            wcyc = b * E + e
            if e + 1 < E:
                for g in range(hcg_n):
                    w1_plan.append((e + 1, g))
                    w1_free_tok.append(None)
            pump_w1()
            if wcyc >= 1:
                wait_tok(SC, hs_free_tok[wcyc - 1])
            for g in range(hcg_n):
                dma(SC, w2sb[:, g * hcg * D:(g + 1) * hcg * D], w2s_h[e, g], s_w2)
            W2_V = s_w2.n
            pump(2)

            for hc in range(hc_n):
                g, hcl = hc // hcg, hc % hcg
                sl = hc % 2
                if hc == 0:
                    PE.wait_ge(s_psbf.h, TRNC_V)
                if sl == 1 and e == 0 and hc == 1:
                    wait_tok(PE, stg_free["B"])
                if hcl == 0:
                    wait_tok(PE, w1_load_tok[wcyc * hcg_n + g])
                k1 = len(ps1_free_at)
                if k1 >= 2:
                    PE.wait_ge(s_hs.h, ps1_free_at[k1 - 2])
                last = None
                for dc in range(dc_n):
                    o = hcl * dc_n * 128 + dc * 128
                    last = PE.matmul(out=ps1(sl, CAP),
                                     lhsT=w1buf[(wcyc * hcg_n + g) % 2][:, o:o + 128],
                                     rhs=x1tg[1][:, dc * CAP:dc * CAP + CAP],
                                     start=(dc == 0), stop=(dc == dc_n - 1))
                s_ps1r.inc(last)
                if hcl == hcg - 1:
                    w1_free_tok[wcyc * hcg_n + g] = (s_ps1r, s_ps1r.n)
                    pump_w1()
                SC.wait_ge(s_ps1r.h, s_ps1r.n)
                if hc == 0 and wcyc >= 1:
                    wait_tok(SC, hs_free_tok[wcyc - 1])
                if hc == 0 and wcyc == E and ffn_state["fadd_last"] is not None:
                    wait_tok(SC, ffn_state["fadd_last"])
                ist = SC.activation(out=hs[:, hc * CAP:(hc + 1) * CAP], in_=ps1(sl, CAP),
                                    func=AF.Relu, bias=b1_sb[:, hc * E + e:hc * E + e + 1])
                ps1_free_at.append(s_hs.inc(ist))
                if hc % 2 == 1:
                    pump(2)
            HS_V = s_hs.n
            w1ph_tok = (s_ps1r, s_ps1r.n)

            PE.wait_ge(s_w2.h, W2_V)
            for tb in range(spc):
                sl2 = tb % 2
                wait_tok(PE, ps2_free[sl2])
                last = None
                for hc in range(hc_n):
                    if tb == 0:
                        PE.wait_ge(s_hs.h, HS_V - hc_n + hc + 1)
                    for half in range(dhalf_n):
                        last = PE.matmul(out=PSF[:, sl2 * 1024 + half * 512:sl2 * 1024 + half * 512 + dhw],
                                         lhsT=hs[:, hc * CAP + tb * 128:hc * CAP + (tb + 1) * 128],
                                         rhs=w2sb[:, hc * D + half * 512:hc * D + half * 512 + dhw],
                                         start=(hc == 0), stop=(hc == hc_n - 1))
                s_ps2r.inc(last)
                if tb == spc - 1:
                    hs_free_tok.append((s_ps2r, s_ps2r.n))
                SC.wait_ge(s_ps2r.h, s_ps2r.n)
                wait_tok(SC, ogw_tok[0])
                ist = SC.copy(out=ogst[:], in_=PSF[:, sl2 * 1024:sl2 * 1024 + D])
                s_ogc.inc(ist)
                ps2_free[sl2] = s_ogc.tok()
                SY.wait_ge(s_ogc.h, s_ogc.n)
                dma(SY, og_h[e * CAP + tb * 128:e * CAP + (tb + 1) * 128, :],
                    ogst[:], s_ogw[0])
                ogw_tok[0] = s_ogw[0].tok()
                if e + 1 < E:
                    t_round(tb, tb == 0, xgl_v, w1ph_tok)
                pump(2)
            if e + 1 < E:
                trn_tok = (s_psbr, s_psbr.n)
                TRNC_V = s_psbf.n
                if e + 2 < E:
                    xgl_v = load_xg(e + 2, trn_tok)
        ffn_state["xgl_all"] = (s_xgl, s_xgl.n)
        x1tg_free[0] = trn_tok
        x1tg_free[1] = w1ph_tok

        pump(900)
        OGW_V = (s_ogw[0].n, s_ogw[1].n)
        HSR_TOK = hs_free_tok[-1]
        fadd_tok = [None] * tb_n
        for tile in range(tb_n):
            par = tile % 2
            c0 = pcol0 + tile * 3
            at = acc[:, tile * D:(tile + 1) * D]
            if tile <= 1:
                GP.wait_ge(s_ogw[0].h, OGW_V[0])
                GP.wait_ge(s_ogw[1].h, OGW_V[1])
                GP.wait_ge(s_zf.h, ZF_ALL)
                wait_tok(GP, HSR_TOK)
            GP.wait_ge(s_rte.h, PTR_V[tile])
            if tile >= 2:
                wait_tok(GP, fadd_tok[tile - 2])
            for j in range(3):
                ist = GP.indirect_dma_start(
                    out=hs[:, (par * 3 + j) * D:(par * 3 + j + 1) * D], out_offset=None,
                    in_=og_h, in_offset=IndirectOffsetOnAxis(ptr_u[:, c0 + j:c0 + j + 1], 0))
                s_og3[par].inc(ist, 16)
            OG3_V = s_og3[par].n
            if tile == 0:
                wait_tok(PE, GT_TOK)
            wait_tok(PE, ps2_free[par])
            last = None
            for half in range(dhalf_n):
                last = PE.matmul(out=PSF[:, par * 1024 + half * 512:par * 1024 + half * 512 + dhw],
                                 lhsT=gT_sb[gp_][:, tile * 128:(tile + 1) * 128],
                                 rhs=b2_sb[:, half * 512:half * 512 + dhw],
                                 start=True, stop=True)
            s_ps2r.inc(last)
            B2_V = s_ps2r.n
            VE.wait_ge(s_ps2r.h, B2_V)
            if tile == 0:
                VE.wait_ge(s_xsc.h, XSC_ALL)
            s_vch.inc(VE.tensor_tensor(out=at, in0=at, in1=PSF[:, par * 1024:par * 1024 + D],
                                       op=ALU.add))
            ps2_free[par] = s_vch.tok()
            VE.wait_ge(s_og3[par].h, OG3_V)
            for j in range(3):
                VE.wait_ge(s_vch.h, s_vch.n)
                ist = VE.scalar_tensor_tensor(out=at, in0=hs[:, (par * 3 + j) * D:(par * 3 + j + 1) * D],
                                              scalar=gval_sb[:, c0 + j:c0 + j + 1], in1=at,
                                              op0=ALU.mult, op1=ALU.add)
                if j == 2:
                    s_accu.inc(ist)
                else:
                    s_vch.inc(ist)
            fadd_tok[tile] = (s_accu, s_accu.n)
            ln2t = hs[:, 6 * D:7 * D]
            VE.wait_ge(s_accu.h, s_accu.n)
            s_vch.inc(VE.reduce_sum(out=c_sum, in_=at, axis=AX.X))
            s_vch.inc(VE.tensor_tensor(out=ln2t, in0=at, in1=at, op=ALU.mult))
            VE.wait_ge(s_vch.h, s_vch.n)
            s_vch.inc(VE.reduce_sum(out=c_sq, in_=ln2t, axis=AX.X))
            VE.wait_ge(s_vch.h, s_vch.n)
            s_vch.inc(VE.tensor_scalar(out=c_mean, in0=c_sum, scalar1=1.0 / D, scalar2=None, op0=ALU.mult))
            s_vch.inc(VE.tensor_scalar(out=c_var, in0=c_sq, scalar1=1.0 / D, scalar2=None, op0=ALU.mult))
            VE.wait_ge(s_vch.h, s_vch.n)
            s_vch.inc(VE.tensor_tensor(out=c_m2, in0=c_mean, in1=c_mean, op=ALU.mult))
            VE.wait_ge(s_vch.h, s_vch.n)
            s_ln2v.inc(VE.tensor_tensor(out=c_var, in0=c_var, in1=c_m2, op=ALU.subtract))
            SC.wait_ge(s_ln2v.h, s_ln2v.n)
            s_sch.inc(SC.activation(out=c_lnv, in_=c_var, func=AF.Ln, bias=1e-5))
            SC.wait_ge(s_sch.h, s_sch.n)
            s_ln2s.inc(SC.activation(out=c_rstd, in_=c_lnv, func=AF.Exp, scale=-0.5))
            VE.wait_ge(s_ln2s.h, s_ln2s.n)
            if s_outv.n >= 1:
                VE.wait_ge(s_st[0].h, s_st[0].n)
            s_outv.inc(VE.tensor_scalar(out=out_sb[0][:], in0=at, scalar1=c_mean,
                                        scalar2=c_rstd, op0=ALU.subtract, op1=ALU.mult))
            SY.wait_ge(s_outv.h, s_outv.n)
            dma(SY, out_h[b, tile * 128:(tile + 1) * 128, :], out_sb[0][:], s_st[0])
            pump(2)
        ffn_state["fadd_last"] = (s_outv, s_outv.n)

    def drain(g):
        for _ in g:
            pass

    def pump_n(g, k):
        for _ in range(k):
            if next(g, 'DONE') == 'DONE':
                return False
        return True

    p0 = p0_gen()
    m0 = mamba_gen(0)
    alive_p, alive_m = True, True
    while alive_p or alive_m:
        if alive_m:
            alive_m = pump_n(m0, 6)
        if alive_p:
            alive_p = pump_n(p0, 1)
    P0W2_ALL = s_p0w2.n
    m1 = mamba_gen(1) if B > 1 else None
    def pump1(k=1):
        if m1 is not None:
            pump_n(m1, k)
    pump1(48)
    ffn_emit(0, pump1)
    if B > 1:
        drain(m1)
        ffn_emit(1, lambda k=1: None)

    SY.wait_ge(s_st[0].h, s_st[0].n)
    SY.wait_ge(s_st[1].h, s_st[1].n)
    ctx.close()
    return nc


def aux_inputs(cfg):
    D, E, tb_n, CAP = cfg.D, cfg.E, cfg.tb_n, cfg.CAP
    fc = np.zeros((128, 8), np.float32)
    fc[:, 0] = 1e-5
    fc[:, 1] = -0.5
    fc[:, 2] = np.float32(1.0) / np.float32(D)
    fc[:, 3] = np.float32(E * CAP)
    lt = np.triu(np.ones((tb_n + 1, tb_n), np.float32), 1)
    lt[tb_n, :] = 1.0
    return {
        "aux_fconst": fc,
        "aux_idn32": np.eye(128, dtype=np.float32),
        "aux_idnb": np.eye(128).astype(ml_dtypes.bfloat16),
        "aux_onesc": np.ones((128, 1), np.float32),
        "aux_onescb": np.ones((128, 1)).astype(ml_dtypes.bfloat16),
        "aux_ltri": np.triu(np.ones((128, 128), np.float32), 1),
        "aux_lt": lt,
        "aux_ecap": (np.arange(E, dtype=np.float32) * CAP).reshape(1, E),
        "aux_onesE": np.ones((128, E), np.float32),
        "aux_zrow": np.zeros((128, D), ml_dtypes.bfloat16),
    }


def make_in_map(cfg, inputs, xslice):
    m = {}
    m["L0_attn_sequence"] = np.ascontiguousarray(xslice)
    for k in ("L0_attn_A", "L0_attn_W_B", "L0_attn_W_C", "L0_attn_W_Delta",
              "L0_ffn_gate", "L0_ffn_W1", "L0_ffn_b1", "L0_ffn_W2", "L0_ffn_b2"):
        m[k] = np.ascontiguousarray(inputs[k])
    m["L0_attn_b_B"] = np.ascontiguousarray(inputs["L0_attn_b_B"]).reshape(cfg.N, 1)
    m["L0_attn_b_C"] = np.ascontiguousarray(inputs["L0_attn_b_C"]).reshape(cfg.N, 1)
    m["L0_attn_b_Delta"] = np.ascontiguousarray(inputs["L0_attn_b_Delta"]).reshape(1, 1)
    m["L0_attn_bias_Delta"] = np.ascontiguousarray(inputs["L0_attn_bias_Delta"]).reshape(1, 1)
    m.update(aux_inputs(cfg))
    return m


def kernel(**inputs):
    from concourse.bass_utils import run_bass_kernel_spmd
    cfg = CFG(B=2, T=2048, D=1024, H=2048)
    nc = bass.Bass("TRN2", target_bir_lowering=False)
    build(nc, cfg)
    x = inputs["L0_attn_sequence"]
    in_maps = [make_in_map(cfg, inputs, x[2 * c:2 * c + 2]) for c in range(8)]
    res = run_bass_kernel_spmd(nc, in_maps, core_ids=list(range(8)))
    return np.concatenate([r["out"] for r in res.results], axis=0)


# version 17
# speedup vs baseline: 1.1468x; 1.1468x over previous
import numpy as np
import ml_dtypes
from contextlib import ExitStack

import concourse.bass as bass
import concourse.mybir as mybir

F32 = mybir.dt.float32
BF16 = mybir.dt.bfloat16
AF = mybir.ActivationFunctionType
ALU = mybir.AluOpType
AX = mybir.AxisListType


class CFG:
    def __init__(self, B=2, T=2048, D=1024, H=2048):
        self.B, self.T, self.D, self.H = B, T, D, H
        self.E = 16
        self.N = 16
        self.TC = 512
        assert T % self.TC == 0 and D % 128 == 0 and H % 128 == 0
        self.tc_n = T // self.TC
        self.dc_n = D // 128
        self.hc_n = H // 128
        self.CAP = 512 if T == 2048 else T // 4 + 128
        self.spc = self.CAP // 128
        self.hcg = min(2, self.hc_n)
        assert self.hc_n % self.hcg == 0
        self.hcg_n = self.hc_n // self.hcg
        self.tb_n = T // 128
        self.dhalf_n = max(1, D // 512)
        self.dhw = min(512, D)


class Sem:
    __slots__ = ("h", "n", "name")

    def __init__(self, ctx, nc, name):
        self.h = ctx.enter_context(nc.semaphore(name))
        self.n = 0
        self.name = name

    def inc(self, instr, k=1):
        instr.then_inc(self.h, k)
        self.n += k
        return self.n

    def tok(self):
        return (self, self.n)


def wait_tok(eng, tok):
    if tok is not None:
        eng.wait_ge(tok[0].h, tok[1])


def build(nc, cfg: CFG):
    B, T, D, H, E, N = cfg.B, cfg.T, cfg.D, cfg.H, cfg.E, cfg.N
    TC, tc_n, dc_n, hc_n, hcg, hcg_n, tb_n = (
        cfg.TC, cfg.tc_n, cfg.dc_n, cfg.hc_n, cfg.hcg, cfg.hcg_n, cfg.tb_n)
    tg_n = tc_n
    tpc = TC // 128
    dhalf_n, dhw = cfg.dhalf_n, cfg.dhw
    CAP, spc = cfg.CAP, cfg.spc
    from concourse.bass import IndirectOffsetOnAxis

    x_h = nc.dram_tensor("L0_attn_sequence", [B, T, D], F32, kind="ExternalInput").ap()
    A_h = nc.dram_tensor("L0_attn_A", [D, N], F32, kind="ExternalInput").ap()
    WB_h = nc.dram_tensor("L0_attn_W_B", [N, D], F32, kind="ExternalInput").ap()
    bB_h = nc.dram_tensor("L0_attn_b_B", [N, 1], F32, kind="ExternalInput").ap()
    WC_h = nc.dram_tensor("L0_attn_W_C", [N, D], F32, kind="ExternalInput").ap()
    bC_h = nc.dram_tensor("L0_attn_b_C", [N, 1], F32, kind="ExternalInput").ap()
    WD_h = nc.dram_tensor("L0_attn_W_Delta", [1, D], F32, kind="ExternalInput").ap()
    bD_h = nc.dram_tensor("L0_attn_b_Delta", [1, 1], F32, kind="ExternalInput").ap()
    biasD_h = nc.dram_tensor("L0_attn_bias_Delta", [1, 1], F32, kind="ExternalInput").ap()
    gate_h = nc.dram_tensor("L0_ffn_gate", [D, E], F32, kind="ExternalInput").ap()
    W1_h = nc.dram_tensor("L0_ffn_W1", [E, D, H], F32, kind="ExternalInput").ap()
    b1_h = nc.dram_tensor("L0_ffn_b1", [E, H], F32, kind="ExternalInput").ap()
    W2_h = nc.dram_tensor("L0_ffn_W2", [E, H, D], F32, kind="ExternalInput").ap()
    b2_h = nc.dram_tensor("L0_ffn_b2", [E, D], F32, kind="ExternalInput").ap()
    idn32_h = nc.dram_tensor("aux_idn32", [128, 128], F32, kind="ExternalInput").ap()
    idnb_h = nc.dram_tensor("aux_idnb", [128, 128], BF16, kind="ExternalInput").ap()
    onesc_h = nc.dram_tensor("aux_onesc", [128, 1], F32, kind="ExternalInput").ap()
    onescb_h = nc.dram_tensor("aux_onescb", [128, 1], BF16, kind="ExternalInput").ap()
    fconst_h = nc.dram_tensor("aux_fconst", [128, 8], F32, kind="ExternalInput").ap()
    ltri_h = nc.dram_tensor("aux_ltri", [128, 128], F32, kind="ExternalInput").ap()
    lt_h = nc.dram_tensor("aux_lt", [cfg.tb_n + 1, cfg.tb_n], F32, kind="ExternalInput").ap()
    ecap_h = nc.dram_tensor("aux_ecap", [1, E], F32, kind="ExternalInput").ap()
    onesE_h = nc.dram_tensor("aux_onesE", [128, E], F32, kind="ExternalInput").ap()
    zrow_h = nc.dram_tensor("aux_zrow", [128, D], BF16, kind="ExternalInput").ap()

    out_h = nc.dram_tensor("out", [B, T, D], F32, kind="ExternalOutput").ap()

    w1s_h = nc.dram_tensor("w1s", [E, hcg_n, 128, hcg * dc_n * 128], BF16, kind="Internal").ap()
    w2s_h = nc.dram_tensor("w2s", [E, hcg_n, 128, hcg * D], BF16, kind="Internal").ap()
    x1d_h = nc.dram_tensor("x1d", [B, dc_n, 128, T], BF16, kind="Internal").ap()
    rows_h = nc.dram_tensor("rowsb", [2 * N + 1, TC], F32, kind="Internal").ap()
    rows16_h = nc.dram_tensor("rowsb16", [2 * N, TC], BF16, kind="Internal").ap()
    lnr_h = nc.dram_tensor("lnrb", [2, TC], F32, kind="Internal").ap()
    xg_h = nc.dram_tensor("xg", [E * CAP + 1, D], BF16, kind="Internal").ap()
    og_h = nc.dram_tensor("og", [E * CAP + 1, D], BF16, kind="Internal").ap()
    cntd_h = nc.dram_tensor("cntd", [tb_n, E], F32, kind="Internal").ap()
    based_h = nc.dram_tensor("based", [tb_n, E], F32, kind="Internal").ap()

    ctx = ExitStack()
    e_ = ctx.enter_context

    idn32 = e_(nc.sbuf_tensor("idn32", [128, 128], F32))
    idnb = e_(nc.sbuf_tensor("idnb", [128, 128], BF16))
    onesc = e_(nc.sbuf_tensor("onesc", [128, 1], F32))
    onescb = e_(nc.sbuf_tensor("onescb", [128, 1], BF16))
    A_sb = e_(nc.sbuf_tensor("A_sb", [128, dc_n * N], F32))
    invA_sb = e_(nc.sbuf_tensor("invA_sb", [128, dc_n * N], F32))
    gate_sb = e_(nc.sbuf_tensor("gate_sb", [128, dc_n * E], F32))
    wbcd_sb = e_(nc.sbuf_tensor("wbcd_sb", [128, dc_n * 33], F32))
    bias33 = e_(nc.sbuf_tensor("bias33", [33, 1], F32))
    biasd_tmp = e_(nc.sbuf_tensor("biasd_tmp", [33, 1], F32))
    b1_sb = e_(nc.sbuf_tensor("b1_sb", [128, hc_n * E], F32))
    b2_sb = e_(nc.sbuf_tensor("b2_sb", [E, D], BF16))
    fc_sb = e_(nc.sbuf_tensor("fc_sb", [128, 8], F32))
    ltri_sb = e_(nc.sbuf_tensor("ltri_sb", [128, 128], F32))
    lt_sb = e_(nc.sbuf_tensor("lt_sb", [tb_n + 1, tb_n], F32))
    onesE_sb = e_(nc.sbuf_tensor("onesE_sb", [128, E], F32))
    cntr_sb = e_(nc.sbuf_tensor("cntr_sb", [tb_n + 1, E], F32))
    nc.const_aps.register(F32, 1e-5, fc_sb[:, 0:1])
    nc.const_aps.register(F32, -0.5, fc_sb[:, 1:2])
    nc.const_aps.register(F32, 1.0 / D, fc_sb[:, 2:3])

    PSF = e_(nc.psum_tensor("PSF", [128, 3584], F32))
    PSB = e_(nc.psum_tensor("PSB", [128, 1024], BF16))

    def ps1(slot, w=None):
        w = TC if w is None else w
        return PSF[:, 2048 + slot * 512:2048 + slot * 512 + w]

    def stgA(p0, p1, c0, c1):
        return PSF[p0:p1, 3072 + c0:3072 + c1]

    def stgB(p0, p1, c0, c1):
        return PSF[p0:p1, 2560 + c0:2560 + c1]

    def S(name):
        return Sem(ctx, nc, name)

    s_ld = S("ld"); s_ldg = S("ldg"); s_tab = S("tab")
    s_p0l = S("p0l"); s_p0c = S("p0c")
    s_p0w = [S("p0wa"), S("p0wb")]; s_p0w2 = S("p0w2")
    s_xn = [S("xn0"), S("xn1")]
    s_stgAr = S("stgAr"); s_stgAf = S("stgAf")
    s_stgBr = S("stgBr"); s_stgBf = S("stgBf")
    s_bcd = S("bcd"); s_row = S("row")
    s_dd = S("dd"); s_bm = [S("bm0"), S("bm1")]; s_cm = [S("cm0"), S("cm1")]; s_lnw = S("lnw")
    s_row16 = S("row16")
    s_ab = S("ab"); s_bx = S("bx"); s_in1 = S("in1"); s_scan = S("scan"); s_icm = S("icm")
    s_carry = S("carry"); s_yt = S("yt"); s_yadd = S("yadd"); s_yinit = S("yinit")
    s_sq = S("sq"); s_sqmm = S("sqmm"); s_lnr = S("lnr")
    s_x1n = S("x1n"); s_x1 = S("x1"); s_x1d = [S("x1d0"), S("x1d1")]; s_x1l = [S("x1l0"), S("x1l1")]
    s_gv = S("gv"); s_gs = S("gs"); s_g = S("g")
    s_w1 = [S("w1a"), S("w1b")]; s_w2 = S("w2")
    s_ps1r = S("ps1r"); s_hs = S("hs")
    s_ps2r = S("ps2r"); s_accu = S("accu"); s_acci = S("acci")
    s_psbr = S("psbr"); s_psbf = S("psbf")
    s_ln2v = S("ln2v"); s_ln2s = S("ln2s"); s_outv = S("outv")
    s_vch = S("vch"); s_sch = S("sch"); s_gch = S("gch")
    s_st = [S("st0"), S("st1")]
    s_rte = S("rte"); s_cb = S("cb"); s_bas = [S("bas0"), S("bas1")]
    s_xsc = S("xsc"); s_xgl = S("xgl"); s_zf = S("zf")
    s_ogc = S("ogc"); s_ogw = [S("ogw0"), S("ogw1")]
    s_og3 = [S("og30"), S("og31")]

    SY, SC, VE, PE, GP = nc.sync, nc.scalar, nc.vector, nc.tensor, nc.gpsimd

    blk = e_(nc.Block())

    def dma(eng, out, in_, sem):
        eng.dma_start(out=out, in_=in_).then_inc(sem.h, 16)
        sem.n += 16
        return sem.n

    stg_free = {"A": None, "B": None}

    def stage(bank, m_rows, m_cols, pe_emit, ev_eng, dst, ev_sem):
        reg, rs = (stgA, s_stgAr) if bank == "A" else (stgB, s_stgBr)
        wait_tok(PE, stg_free[bank])
        rs.inc(pe_emit())
        ev_eng.wait_ge(rs.h, rs.n)
        ist = ev_eng.copy(out=dst, in_=reg(0, m_rows, 0, m_cols))
        ev_sem.inc(ist)
        stg_free[bank] = ev_sem.tok()
        return ev_sem.tok()

    tabctx = ExitStack()
    wbt = tabctx.enter_context(nc.sbuf_tensor("wbt", [N, D], F32))
    wct = tabctx.enter_context(nc.sbuf_tensor("wct", [N, D], F32))
    wdt = tabctx.enter_context(nc.sbuf_tensor("wdt", [1, D], F32))
    b1t = tabctx.enter_context(nc.sbuf_tensor("b1t", [E, H], F32))
    dma(SY, idn32[:], idn32_h, s_ld)
    dma(SY, idnb[:], idnb_h, s_ld)
    dma(SY, onesc[:], onesc_h, s_ld)
    dma(SY, onescb[:], onescb_h, s_ld)
    dma(SY, fc_sb[:], fconst_h, s_ld)
    for dc in range(dc_n):
        dma(SY, A_sb[:, dc * N:(dc + 1) * N], A_h[dc * 128:(dc + 1) * 128, :], s_ld)
        dma(SY, gate_sb[:, dc * E:(dc + 1) * E], gate_h[dc * 128:(dc + 1) * 128, :], s_ld)
    dma(SY, wbt[:], WB_h, s_ld)
    dma(SY, wct[:], WC_h, s_ld)
    dma(SY, wdt[:], WD_h, s_ld)
    dma(SY, b1t[:], b1_h, s_ld)
    dma(GP, b2_sb[:], b2_h, s_ldg)
    dma(SY, ltri_sb[:], ltri_h, s_ld)
    dma(SY, lt_sb[:], lt_h, s_ld)
    dma(SY, onesE_sb[:], onesE_h, s_ld)
    dma(SY, cntr_sb[tb_n:tb_n + 1, :], ecap_h, s_ld)
    for i in range(E * CAP // 128):
        dma(GP, xg_h[i * 128:(i + 1) * 128, :], zrow_h, s_zf)
    dma(GP, og_h[E * CAP:E * CAP + 1, :], zrow_h[0:1, :], s_zf)
    ZF_ALL = s_zf.n
    dma(SY, bias33[0:N, 0:1], bB_h, s_ld)
    dma(SY, bias33[N:2 * N, 0:1], bC_h, s_ld)
    dma(SY, bias33[32:33, 0:1], bD_h, s_ld)
    dma(SY, biasd_tmp[32:33, 0:1], biasD_h, s_ld)
    LD_ALL = s_ld.n

    VE.wait_ge(s_ld.h, LD_ALL)
    VE.reciprocal(out=invA_sb[:], in_=A_sb[:])
    s_tab.inc(VE.tensor_tensor(out=bias33[32:33, 0:1], in0=bias33[32:33, 0:1],
                               in1=biasd_tmp[32:33, 0:1], op=ALU.add))

    PE.wait_ge(s_ld.h, LD_ALL)
    tab_jobs = []
    for dc in range(dc_n):
        slc = slice(dc * 128, (dc + 1) * 128)
        tab_jobs.append((wbt[:, slc], N, wbcd_sb[:, dc * 33:dc * 33 + N]))
        tab_jobs.append((wct[:, slc], N, wbcd_sb[:, dc * 33 + N:dc * 33 + 2 * N]))
        tab_jobs.append((wdt[:, slc], 1, wbcd_sb[:, dc * 33 + 32:dc * 33 + 33]))
    for hc in range(hc_n):
        tab_jobs.append((b1t[:, hc * 128:(hc + 1) * 128], E, b1_sb[:, hc * E:(hc + 1) * E]))
    for src, m, dst in tab_jobs:
        kk = src.shape[0]
        stage("A", 128, m,
              lambda src=src, m=m, kk=kk: PE.transpose(out=stgA(0, 128, 0, m), in_=src,
                                                       identity=idn32[0:kk, 0:kk]),
              SC, dst, s_stgAf)
    s_tab.inc(SC.copy(out=biasd_tmp[0:1, 0:1], in_=bias33[0:1, 0:1]))
    TAB_ALL = s_tab.n
    tabctx.close()

    def p0_gen():
        for e in range(E):
            for g in range(hcg_n):
                src = W2_h[e, g * hcg * 128:(g + 1) * hcg * 128, :]
                src = src.rearrange("(hcl p) j -> p hcl j", p=128)
                dst = w2s_h[e, g].rearrange("p (hcl j) -> p hcl j", hcl=hcg)
                dma(GP, dst, src, s_p0w2)
            yield
        w1tmp = [w2sb[:, dc * H:(dc + 1) * H] for dc in range(dc_n)]
        asmw = hcg * dc_n * 128
        asm1 = [hs[:, i * asmw:(i + 1) * asmw] for i in range(2)]
        cpe = hc_n * dc_n
        GP.wait_ge(s_tab.h, TAB_ALL)
        for e in range(E):
            for dc in range(dc_n):
                if e > 0:
                    GP.wait_ge(s_p0c.h, e * cpe)
                dma(GP, w1tmp[dc], W1_h[e, dc * 128:(dc + 1) * 128, :], s_p0l)
            yield
            for g in range(hcg_n):
                gi = e * hcg_n + g
                a = asm1[gi % 2]
                VE.wait_ge(s_p0l.h, s_p0l.n)
                if gi >= 2:
                    VE.wait_ge(s_p0w[gi % 2].h, s_p0w[gi % 2].n)
                last = None
                for hcl in range(hcg):
                    hc = g * hcg + hcl
                    for dc in range(dc_n):
                        o = hcl * dc_n * 128 + dc * 128
                        last = VE.tensor_copy(out=a[:, o:o + 128],
                                              in_=w1tmp[dc][:, hc * 128:(hc + 1) * 128])
                s_p0c.inc(last, hcg * dc_n)
                SY.wait_ge(s_p0c.h, s_p0c.n)
                dma(SY, w1s_h[e, g], a, s_p0w[gi % 2])
                if g % 2 == 1:
                    yield

    xn = [e_(nc.sbuf_tensor(f"xn{i}", [128, D], F32)) for i in range(1)]
    xc = [e_(nc.sbuf_tensor(f"xc{i}", [128, TC], F32)) for i in range(dc_n)]
    y_sb = [e_(nc.sbuf_tensor(f"y{i}", [128, TC], F32)) for i in range(dc_n)]
    bcd_sb = e_(nc.sbuf_tensor("bcd_sb", [33, TC], F32))
    sp_tmp = e_(nc.sbuf_tensor("sp_tmp", [33, TC], F32))
    drep = e_(nc.sbuf_tensor("drep", [128, TC], F32))
    bmrep = [e_(nc.sbuf_tensor(f"bmrep{i}", [128, TC], BF16)) for i in range(2)]
    cmrep = [e_(nc.sbuf_tensor(f"cmrep{i}", [128, TC], BF16)) for i in range(2)]
    icm_b = [e_(nc.sbuf_tensor(f"icmb{i}", [128, TC], BF16)) for i in range(2)]
    rrep32 = e_(nc.sbuf_tensor("rrep32", [128, TC], F32))
    ln1tmp = e_(nc.sbuf_tensor("ln1tmp", [128, TC], F32))
    ab_b = [e_(nc.sbuf_tensor(f"abb{i}", [128, TC], BF16)) for i in range(6)]
    bx_b = [e_(nc.sbuf_tensor(f"bxb{i}", [128, TC], BF16)) for i in range(4)]
    in1_b = [e_(nc.sbuf_tensor(f"in1b{i}", [128, TC], BF16)) for i in range(2)]
    sc_b = [e_(nc.sbuf_tensor(f"scb{i}", [128, TC], F32)) for i in range(2)]
    yt_b = [e_(nc.sbuf_tensor(f"ytb{i}", [128, TC], BF16)) for i in range(4)]
    sq_b = in1_b
    carry = e_(nc.sbuf_tensor("carry", [128, dc_n * N], F32))
    lnrow = e_(nc.sbuf_tensor("lnrow", [1, 2 * TC], F32))
    eg_sb = e_(nc.sbuf_tensor("eg_sb", [128, E], F32))
    m8_sb = e_(nc.sbuf_tensor("m8_sb", [128, 8], F32))
    mask_sb = e_(nc.sbuf_tensor("mask_sb", [128, E], F32))
    gsel_sb = e_(nc.sbuf_tensor("gsel_sb", [128, E], F32))
    cols = e_(nc.sbuf_tensor("cols", [128, 16], F32))
    g_sb = [e_(nc.sbuf_tensor(f"g_sb{i}", [128, tb_n * E], F32)) for i in range(2)]
    gT_sb = [e_(nc.sbuf_tensor(f"gT_sb{i}", [E, T], BF16)) for i in range(2)]
    x1stg = [e_(nc.sbuf_tensor(f"x1stg{i}", [128, TC], BF16)) for i in range(1)]
    x1tg = [e_(nc.sbuf_tensor(f"x1tg{i}", [128, dc_n * TC], BF16)) for i in range(2)]
    acc = e_(nc.sbuf_tensor("acc", [128, tb_n * D], BF16))
    hs = e_(nc.sbuf_tensor("hs", [128, max(hc_n * TC, 7 * D)], BF16))
    mskt_sb = e_(nc.sbuf_tensor("mskt_sb", [128, E], F32))
    srank_sb = e_(nc.sbuf_tensor("srank_sb", [128, E], F32))
    cnt_sb = e_(nc.sbuf_tensor("cnt_sb", [1, tb_n * E], F32))
    bs_sb = e_(nc.sbuf_tensor("bs_sb", [tb_n, E], F32))
    basebc = e_(nc.sbuf_tensor("basebc", [128, 2 * E], F32))
    rt_sb = e_(nc.sbuf_tensor("rt_sb", [128, 5 * E + 3], F32))
    ptr_u = e_(nc.sbuf_tensor("ptr_u", [128, 2 * tb_n * 3], mybir.dt.uint32))
    gval_sb = e_(nc.sbuf_tensor("gval_sb", [128, 2 * tb_n * 3], F32))
    ogst = e_(nc.sbuf_tensor("ogst", [128, D], BF16))
    w1buf = [e_(nc.sbuf_tensor(f"w1buf{i}", [128, hcg * dc_n * 128], BF16)) for i in range(2)]
    w2sb = e_(nc.sbuf_tensor("w2sb", [128, hc_n * D], BF16))
    out_sb = [e_(nc.sbuf_tensor(f"out_sb{i}", [128, D], F32)) for i in range(1)]

    c_rowmax = cols[:, 0:1]; c_negmax = cols[:, 1:2]
    c_den = cols[:, 2:3]; c_den2 = cols[:, 3:4]; c_rec = cols[:, 4:5]
    c_sum = cols[:, 5:6]; c_sq = cols[:, 6:7]; c_mean = cols[:, 7:8]
    c_m2 = cols[:, 8:9]; c_var = cols[:, 9:10]; c_lnv = cols[:, 10:11]; c_rstd = cols[:, 11:12]

    for ENG in (PE, VE, SC, GP, SY):
        ENG.wait_ge(s_tab.h, TAB_ALL)
        ENG.wait_ge(s_ldg.h, 16)

    carry_val = {}
    acc_last = {}
    w1_load_tok = []
    w1_free_tok = []
    w1_plan = []
    w1_state = {"emitted": 0}

    def pump_w1():
        while w1_state["emitted"] < len(w1_plan):
            k = w1_state["emitted"]
            if k >= 2 and w1_free_tok[k - 2] is None:
                return
            if k >= 2:
                wait_tok(SY, w1_free_tok[k - 2])
            e_, g_ = w1_plan[k]
            dma(SY, w1buf[k % 2][:], w1s_h[e_, g_], s_w1[k % 2])
            w1_load_tok.append((s_w1[k % 2], s_w1[k % 2].n))
            w1_state["emitted"] = k + 1
    ps1_free_at = []
    hs_free_tok = []
    xn_free_tok = []
    rep = {}
    gpe_token = None

    shared = {'gpe': None}
    mstate = {}
    ffn_state = {"x1tg_free": [None, None], "ps2_free": [None, None],
                 "ogw_tok": [None, None], "xgl_all": None, "fadd_last": None}
    def mamba_gen(b):
        gp_ = b % 2
        gpe_token = shared['gpe']
        for tc in range(tc_n):
            t0 = tc * TC
            for tt in range(tpc):
                kx = s_xn[0].n // 16
                if kx >= 1:
                    wait_tok(SY, xn_free_tok[kx - 1])
                slx = 0
                dma(SY, xn[slx][:], x_h[b, t0 + tt * 128:t0 + (tt + 1) * 128, :], s_xn[slx])
                PE.wait_ge(s_xn[slx].h, s_xn[slx].n)
                nuse = (dc_n + 3) // 4
                for u in range(nuse):
                    ndc = min(4, dc_n - u * 4)
                    bank = "A"
                    reg, rs = stgA, s_stgAr
                    fsem = s_stgAf
                    wait_tok(PE, stg_free[bank])
                    last = None
                    for i in range(ndc):
                        dc = u * 4 + i
                        last = PE.transpose(out=reg(0, 128, i * 128, (i + 1) * 128),
                                            in_=xn[slx][:, dc * 128:(dc + 1) * 128],
                                            identity=idn32[:])
                    rs.inc(last)
                    SC.wait_ge(rs.h, rs.n)
                    last2 = None
                    for i in range(ndc):
                        dc = u * 4 + i
                        last2 = SC.copy(out=xc[dc][:, tt * 128:(tt + 1) * 128],
                                        in_=reg(0, 128, i * 128, (i + 1) * 128))
                    fsem.inc(last2)
                    stg_free[bank] = fsem.tok()
                xn_free_tok.append(stg_free["A"])
                yield
            xc_tok = xn_free_tok[-1]

            wait_tok(PE, xc_tok)
            wait_tok(PE, stg_free["A"])
            last = None
            for dc in range(dc_n):
                last = PE.matmul(out=stgA(0, 33, 0, TC), lhsT=wbcd_sb[:, dc * 33:(dc + 1) * 33],
                                 rhs=xc[dc][:, 0:TC], start=(dc == 0), stop=(dc == dc_n - 1))
            s_stgAr.inc(last)
            VE.wait_ge(s_stgAr.h, s_stgAr.n)
            ist = VE.tensor_scalar(out=bcd_sb[:], in0=stgA(0, 33, 0, TC), scalar1=bias33[:],
                                   scalar2=None, op0=ALU.add)
            s_stgAf.inc(ist)
            stg_free["A"] = s_stgAf.tok()
            bcd_tok = s_stgAf.tok()
            z = bcd_sb[32:33, :]
            t1 = sp_tmp[32:33, 0:TC]
            wait_tok(SC, bcd_tok)
            s_sch.inc(SC.activation(out=t1, in_=z, func=AF.Abs))
            SC.wait_ge(s_sch.h, s_sch.n)
            s_sch.inc(SC.activation(out=t1, in_=t1, func=AF.Exp, scale=-1.0))
            SC.wait_ge(s_sch.h, s_sch.n)
            s_sch.inc(SC.activation(out=t1, in_=t1, func=AF.Ln, bias=1.0))
            SC.wait_ge(s_sch.h, s_sch.n)
            s_bcd.inc(SC.activation(out=z, in_=z, func=AF.Relu))
            VE.wait_ge(s_bcd.h, s_bcd.n)
            s_bcd.inc(VE.tensor_tensor(out=z, in0=z, in1=t1, op=ALU.add))
            SY.wait_ge(s_bcd.h, s_bcd.n)
            dma(SY, rows_h, bcd_sb[:], s_row)
            GP.wait_ge(s_bcd.h, s_bcd.n)
            dma(GP, rows16_h, bcd_sb[0:2 * N, :], s_row16)
            ROW_V = s_row.n
            ROW16_V = s_row16.n
            SY.wait_ge(s_row.h, ROW_V)
            SY.wait_ge(s_ab.h, s_ab.n)
            SY.wait_ge(s_x1n.h, s_x1n.n)
            dma(SY, drep[:], rows_h[32:33, :].to_broadcast((128, TC)), s_dd)
            DD_V = s_dd.n
            yield

            wait_tok(GP, xc_tok)
            if b > 0 or tc > 0:
                wait_tok(GP, gpe_token)
                GP.wait_ge(s_x1.h, s_x1.n)
            last = None
            for dc in range(dc_n):
                last = GP.tensor_copy(out=y_sb[dc][:], in_=xc[dc][:])
            s_yinit.inc(last)
            YINIT_V = s_yinit.n

            ylast = {db: None for db in range(dc_n)}
            iters = N * dc_n
            BMv = {}
            CMv = {}

            def emit_repdma(n):
                sl = n % 2
                SY.wait_ge(s_row16.h, ROW16_V)
                SY.wait_ge(s_row.h, ROW_V)
                if n >= 2:
                    SY.wait_ge(s_bx.h, BX_BASE0 + (n - 1) * dc_n)
                    SY.wait_ge(s_icm.h, ICM_BASE0 + (n - 1) * dc_n)
                dma(SY, bmrep[sl][:], rows16_h[n:n + 1, :].to_broadcast((128, TC)), s_bm[sl])
                BMv[n] = s_bm[sl].n
                dma(SY, cmrep[sl][:], rows16_h[N + n:N + n + 1, :].to_broadcast((128, TC)), s_cm[sl])
                CMv[n] = s_cm[sl].n

            BX_BASE0 = s_bx.n
            YT_BASE0 = s_yt.n
            SCAN_BASE0 = s_scan.n
            IN1_BASE0 = s_in1.n
            AB_BASE0 = s_ab.n
            YADD_BASE0 = s_yadd.n
            ICM_BASE0 = s_icm.n

            def fix_rep_wait(n):
                pass

            emit_repdma(0)
            if N > 1:
                emit_repdma(1)
            for k in range(iters + 2):
                if k < iters:
                    n, db = k // dc_n, k % dc_n
                    sl4 = k % 4
                    sl6 = k % 6
                    ci = db * N + n
                    SC.wait_ge(s_dd.h, DD_V)
                    if k >= 6:
                        SC.wait_ge(s_scan.h, SCAN_BASE0 + k - 5)
                    s_ab.inc(SC.activation(out=ab_b[sl6][:], in_=drep[:], func=AF.Exp,
                                           scale=A_sb[:, ci:ci + 1]))
                    SC.wait_ge(s_cm[n % 2].h, CMv[n])
                    if k >= 2:
                        SC.wait_ge(s_yt.h, YT_BASE0 + k - 1)
                    s_icm.inc(SC.activation(out=icm_b[k % 2][:], in_=cmrep[n % 2][:],
                                            func=AF.Copy, scale=invA_sb[:, ci:ci + 1]))
                    GP.wait_ge(s_bm[n % 2].h, BMv[n])
                    if k >= 4:
                        GP.wait_ge(s_in1.h, IN1_BASE0 + k - 3)
                    s_bx.inc(GP.tensor_tensor(out=bx_b[sl4][:], in0=xc[db][:],
                                              in1=bmrep[n % 2][:], op=ALU.mult))
                if k >= 1 and k - 1 < iters:
                    j = k - 1
                    n, db = j // dc_n, j % dc_n
                    sl = j % 2
                    sl4 = j % 4
                    sl6 = j % 6
                    ci = db * N + n
                    VE.wait_ge(s_ab.h, AB_BASE0 + j + 1)
                    VE.wait_ge(s_bx.h, BX_BASE0 + j + 1)
                    s_in1.inc(VE.scalar_tensor_tensor(out=in1_b[sl][:], in0=ab_b[sl6][:], scalar=1.0,
                                                      in1=bx_b[sl4][:], op0=ALU.subtract, op1=ALU.mult))
                    VE.wait_ge(s_in1.h, s_in1.n)
                    if j >= 2:
                        VE.wait_ge(s_yt.h, YT_BASE0 + j - 1)
                    if tc == 0:
                        init = 0.0
                    else:
                        VE.wait_ge(s_carry.h, carry_val[ci])
                        init = carry[:, ci:ci + 1]
                    s_scan.inc(VE.tensor_tensor_scan(out=sc_b[sl][:], data0=ab_b[sl6][:],
                                                     data1=in1_b[sl][:], initial=init,
                                                     op0=ALU.mult, op1=ALU.add))
                    VE.wait_ge(s_scan.h, s_scan.n)
                    if tc < tc_n - 1:
                        carry_val[ci] = s_carry.inc(
                            VE.tensor_copy(out=carry[:, ci:ci + 1], in_=sc_b[sl][:, TC - 1:TC]))
                    VE.wait_ge(s_icm.h, ICM_BASE0 + j + 1)
                    if j >= 4:
                        VE.wait_ge(s_yadd.h, YADD_BASE0 + j - 3)
                    s_yt.inc(VE.tensor_tensor(out=yt_b[sl4][:], in0=sc_b[sl][:],
                                              in1=icm_b[j % 2][:], op=ALU.mult))
                if k >= 2:
                    j = k - 2
                    n, db = j // dc_n, j % dc_n
                    sl4 = j % 4
                    GP.wait_ge(s_yt.h, YT_BASE0 + j + 1)
                    GP.wait_ge(s_yinit.h, YINIT_V)
                    if ylast[db] is not None:
                        GP.wait_ge(s_yadd.h, ylast[db])
                    ylast[db] = s_yadd.inc(GP.tensor_tensor(out=y_sb[db][:], in0=y_sb[db][:],
                                                            in1=yt_b[sl4][:], op=ALU.add))
                if k < iters:
                    n, db = k // dc_n, k % dc_n
                    if db == 0 and 1 <= n and n + 1 < N:
                        emit_repdma(n + 1)
                yield
            YADD_V = s_yadd.n

            PE.wait_ge(s_yadd.h, YADD_V)
            wait_tok(PE, stg_free["A"])
            last = None
            for db in range(dc_n):
                last = PE.matmul(out=stgA(0, 1, 0, TC), lhsT=onesc[:], rhs=y_sb[db][:],
                                 start=(db == 0), stop=(db == dc_n - 1))
            s_stgAr.inc(last)
            SC.wait_ge(s_stgAr.h, s_stgAr.n)
            ist = SC.activation(out=lnrow[0:1, 0:TC], in_=stgA(0, 1, 0, TC), func=AF.Copy, scale=1.0 / D)
            s_stgAf.inc(ist)
            stg_free["A"] = s_stgAf.tok()
            mean_tok = s_stgAf.tok()
            SQMM_BASE = s_sqmm.n
            wait_tok(PE, stg_free["A"])
            for db in range(dc_n):
                VE.wait_ge(s_yadd.h, YADD_V)
                if db >= 1:
                    VE.wait_ge(s_sqmm.h, SQMM_BASE + db)
                s_sq.inc(VE.tensor_tensor(out=sq_b[0][:], in0=y_sb[db][:],
                                          in1=y_sb[db][:], op=ALU.mult))
                PE.wait_ge(s_sq.h, s_sq.n)
                mm = PE.matmul(out=stgA(0, 1, 0, TC), lhsT=onescb[:], rhs=sq_b[0][:],
                               start=(db == 0), stop=(db == dc_n - 1))
                if db == dc_n - 1:
                    s_stgAr.inc(mm)
                else:
                    s_sqmm.inc(mm)
            SC.wait_ge(s_stgAr.h, s_stgAr.n)
            ist = SC.activation(out=lnrow[0:1, TC:2 * TC], in_=stgA(0, 1, 0, TC), func=AF.Copy, scale=1.0 / D)
            s_stgAf.inc(ist)
            stg_free["A"] = s_stgAf.tok()
            msq_tok = s_stgAf.tok()
            wait_tok(SY, mean_tok)
            wait_tok(SY, msq_tok)
            dma(SY, lnr_h.rearrange("r n -> (r n)")[None, :], lnrow[0:1, :], s_lnw)
            SY.wait_ge(s_lnw.h, s_lnw.n)
            SY.wait_ge(s_ab.h, s_ab.n)
            SY.wait_ge(s_x1n.h, s_x1n.n)
            dma(SY, drep[:], lnr_h[0:1, :].to_broadcast((128, TC)), s_bm[0])
            BM_V = s_bm[0].n
            dma(SY, rrep32[:], lnr_h[1:2, :].to_broadcast((128, TC)), s_cm[0])
            CM_V = s_cm[0].n
            VE.wait_ge(s_bm[0].h, BM_V)
            VE.wait_ge(s_cm[0].h, CM_V)
            s_vch.inc(VE.tensor_tensor(out=ln1tmp[:], in0=drep[:], in1=drep[:], op=ALU.mult))
            VE.wait_ge(s_vch.h, s_vch.n)
            s_lnr.inc(VE.tensor_tensor(out=rrep32[:], in0=rrep32[:], in1=ln1tmp[:], op=ALU.subtract))
            SC.wait_ge(s_lnr.h, s_lnr.n)
            s_sch.inc(SC.activation(out=rrep32[:], in_=rrep32[:], func=AF.Ln, bias=1e-5))
            SC.wait_ge(s_sch.h, s_sch.n)
            s_lnr.inc(SC.activation(out=rrep32[:], in_=rrep32[:], func=AF.Exp, scale=-0.5))
            LNR_V = s_lnr.n
            for db in range(dc_n):
                VE.wait_ge(s_lnr.h, LNR_V)
                s_vch.inc(VE.tensor_tensor(out=y_sb[db][:], in0=y_sb[db][:], in1=drep[:], op=ALU.subtract))
                VE.wait_ge(s_vch.h, s_vch.n)
                s_x1n.inc(VE.tensor_tensor(out=y_sb[db][:], in0=y_sb[db][:], in1=rrep32[:], op=ALU.mult))
            yield
            X1N_V = s_x1n.n
            for db in range(dc_n):
                SC.wait_ge(s_x1n.h, X1N_V)
                kq = s_x1.n
                if kq >= 1:
                    SC.wait_ge(s_x1d[0].h, s_x1d[0].n)
                s_x1.inc(SC.copy(out=x1stg[0][:], in_=y_sb[db][:]))
                SY.wait_ge(s_x1.h, s_x1.n)
                dma(SY, x1d_h[b, db, :, t0:t0 + TC], x1stg[0][:], s_x1d[0])
            for tb in range(tpc):
                PE.wait_ge(s_x1n.h, X1N_V)
                wait_tok(PE, stg_free["A"])
                last = None
                for dc in range(dc_n):
                    last = PE.matmul(out=stgA(0, 128, 0, E),
                                     lhsT=y_sb[dc][:, tb * 128:(tb + 1) * 128],
                                     rhs=gate_sb[:, dc * E:(dc + 1) * E],
                                     start=(dc == 0), stop=(dc == dc_n - 1))
                s_stgAr.inc(last)
                VE.wait_ge(s_stgAr.h, s_stgAr.n)
                s_gv.inc(VE.reduce_max(out=c_rowmax, in_=stgA(0, 128, 0, E), axis=AX.X))
                VE.wait_ge(s_gv.h, s_gv.n)
                s_gv.inc(VE.tensor_scalar(out=c_negmax, in0=c_rowmax, scalar1=-1.0,
                                          scalar2=None, op0=ALU.mult))
                SC.wait_ge(s_gv.h, s_gv.n)
                SC.wait_ge(s_g.h, s_g.n)
                ist = SC.activation(out=eg_sb[:], in_=stgA(0, 128, 0, E), func=AF.Exp, bias=c_negmax)
                s_gs.inc(ist)
                stg_free["A"] = s_gs.tok()
                VE.wait_ge(s_gs.h, s_gs.n)
                s_gv.inc(VE.max(out=m8_sb[:], in_=eg_sb[:]))
                VE.wait_ge(s_gv.h, s_gv.n)
                s_gv.inc(VE.tensor_scalar(out=mask_sb[:], in0=eg_sb[:], scalar1=m8_sb[:, 2:3],
                                          scalar2=None, op0=ALU.is_ge))
                VE.wait_ge(s_gv.h, s_gv.n)
                s_gv.inc(VE.tensor_tensor(out=gsel_sb[:], in0=eg_sb[:], in1=mask_sb[:], op=ALU.mult))
                VE.wait_ge(s_gv.h, s_gv.n)
                s_gv.inc(VE.reduce_sum(out=c_den, in_=gsel_sb[:], axis=AX.X))
                VE.wait_ge(s_gv.h, s_gv.n)
                s_gv.inc(VE.tensor_scalar(out=c_den2, in0=c_den, scalar1=1e-12,
                                          scalar2=None, op0=ALU.max))
                VE.wait_ge(s_gv.h, s_gv.n)
                s_gv.inc(VE.reciprocal(out=c_rec, in_=c_den2))
                VE.wait_ge(s_gv.h, s_gv.n)
                gc0 = (tc * tpc + tb) * E
                s_g.inc(VE.tensor_scalar(out=g_sb[gp_][:, gc0:gc0 + E], in0=gsel_sb[:],
                                         scalar1=c_rec, scalar2=None, op0=ALU.mult))
                PE.wait_ge(s_g.h, s_g.n)
                tcol0 = (tc * tpc + tb) * 128
                rep["gT"] = stage("A", E, 128,
                                  lambda gc0=gc0: PE.transpose(out=stgA(0, E, 0, 128),
                                                               in_=g_sb[gp_][:, gc0:gc0 + E],
                                                               identity=idn32[:]),
                                  SC, gT_sb[gp_][:, tcol0:tcol0 + 128], s_stgAf)
                yield
            gpe_token = rep["gT"]
        GT_TOK = rep["gT"]
        G_V = s_g.n
        X1D_V = (s_x1d[0].n, s_x1d[1].n)

        mstate[b] = dict(GT_TOK=GT_TOK, G_V=G_V, X1D_V=X1D_V)
        shared['gpe'] = gpe_token
        yield
    def ffn_emit(b, pump):
        gp_ = b % 2
        if b == 0:
            for ENG in (SY, SC):
                ENG.wait_ge(s_p0w[0].h, s_p0w[0].n)
                ENG.wait_ge(s_p0w[1].h, s_p0w[1].n)
                ENG.wait_ge(s_p0c.h, s_p0c.n)
                ENG.wait_ge(s_p0l.h, s_p0l.n)
                ENG.wait_ge(s_p0w2.h, s_p0w2.n)
        else:
            SC.wait_ge(s_cb.h, s_cb.n)
        GT_TOK = mstate[b]['GT_TOK']
        G_V = mstate[b]['G_V']
        X1D_V = mstate[b]['X1D_V']
        x1tg_free = ffn_state["x1tg_free"]
        ps2_free = ffn_state["ps2_free"]
        ogw_tok = ffn_state["ogw_tok"]
        pcol0 = gp_ * (tb_n * 3)
        HS_GUARD = s_hs.n
        rt0 = rt_sb[:, 0:E]; rt1 = rt_sb[:, E:2 * E]
        rt2 = rt_sb[:, 2 * E:3 * E]; rt3 = rt_sb[:, 3 * E:4 * E]
        rt4 = rt_sb[:, 4 * E:5 * E]
        pf_col = rt_sb[:, 5 * E:5 * E + 1]
        pf2_col = rt_sb[:, 5 * E + 1:5 * E + 2]
        pf3_col = rt_sb[:, 5 * E + 2:5 * E + 3]

        def load_x1tg(tg):
            slg_ = tg % 2
            SY.wait_ge(s_x1d[0].h, X1D_V[0])
            SY.wait_ge(s_x1d[1].h, X1D_V[1])
            wait_tok(SY, x1tg_free[slg_])
            for db in range(dc_n):
                dma(SY, x1tg[slg_][:, db * TC:(db + 1) * TC],
                    x1d_h[b, db, :, tg * TC:tg * TC + TC], s_x1l[slg_])
            x1l_v[slg_] = s_x1l[slg_].n

        x1l_v = [0, 0]
        accp = {}
        rnk_tok = {}
        PTR_V = {}
        load_x1tg(0)
        for tg in range(tg_n):
            slg = tg % 2
            X1L_V = x1l_v[slg]
            for ti in range(tpc):
                tile = tg * tpc + ti
                PE.wait_ge(s_x1l[slg].h, X1L_V)
                PE.wait_ge(s_psbf.h, s_psbf.n)
                last = None
                for dc in range(dc_n):
                    last = PE.transpose(out=PSB[:, dc * 128:(dc + 1) * 128],
                                        in_=x1tg[slg][:, dc * TC + ti * 128:dc * TC + (ti + 1) * 128],
                                        identity=idnb[:])
                s_psbr.inc(last)
                VE.wait_ge(s_psbr.h, s_psbr.n)
                s_psbf.inc(VE.tensor_copy(out=acc[:, tile * D:(tile + 1) * D], in_=PSB[:, 0:D]))
                accp[tile] = s_psbf.tok()
                wait_tok(VE, ffn_state.get("cnt_mm_tok"))
                s_rte.inc(VE.tensor_scalar(out=mskt_sb[:], in0=g_sb[gp_][:, tile * E:(tile + 1) * E],
                                           scalar1=0.0, scalar2=None, op0=ALU.is_gt))
                MSK_V = s_rte.n

                def cnt_mm(MSK_V=MSK_V, first=(b > 0 and tile == 0)):
                    if first:
                        PE.wait_ge(s_hs.h, HS_GUARD)
                    PE.wait_ge(s_rte.h, MSK_V)
                    return PE.matmul(out=stgB(0, 1, 0, E), lhsT=onesc[:], rhs=mskt_sb[:],
                                     start=True, stop=True)
                stage("B", 1, E, cnt_mm, SC, cnt_sb[0:1, tile * E:(tile + 1) * E], s_stgBf)
                ffn_state["cnt_mm_tok"] = (s_stgBr, s_stgBr.n)
                pump(3)
            x1tg_free[slg] = (s_psbr, s_psbr.n)
            if tg + 1 < tg_n:
                load_x1tg(tg + 1)
        CNT_V = s_stgBf.n

        SY.wait_ge(s_stgBf.h, CNT_V)
        dma(SY, cntd_h.rearrange("a e -> (a e)")[None, :], cnt_sb[0:1, :], s_cb)
        SY.wait_ge(s_cb.h, s_cb.n)
        dma(SY, cntr_sb[0:tb_n, :], cntd_h, s_cb)
        CB_V = s_cb.n

        def base_mm():
            PE.wait_ge(s_cb.h, CB_V)
            return PE.matmul(out=stgA(0, tb_n, 0, E), lhsT=lt_sb[:], rhs=cntr_sb[:],
                             start=True, stop=True)
        BS_TOK = stage("A", tb_n, E, base_mm, SC, bs_sb[:], s_stgAf)
        wait_tok(SY, BS_TOK)
        dma(SY, based_h, bs_sb[:], s_cb)
        BD_V = s_cb.n

        bas_rd = ffn_state.setdefault("bas_rd", [None, None])
        rnk_rd = ffn_state.get("rnk_rd")
        srd_tok = ffn_state.get("srd_tok")
        for tile in range(tb_n):
            par = tile % 2
            c0 = pcol0 + tile * 3
            SY.wait_ge(s_cb.h, BD_V)
            wait_tok(SY, bas_rd[par])
            dma(SY, basebc[:, par * E:(par + 1) * E],
                based_h[tile:tile + 1, :].to_broadcast((128, E)), s_bas[par])
            gt_ = g_sb[gp_][:, tile * E:(tile + 1) * E]
            wait_tok(VE, rnk_rd)
            s_rte.inc(VE.tensor_scalar(out=rt4, in0=gt_, scalar1=0.0, scalar2=None, op0=ALU.is_gt))
            MSK2_V = s_rte.n

            def rank_mm(MSK2_V=MSK2_V):
                PE.wait_ge(s_rte.h, MSK2_V)
                return PE.matmul(out=stgA(0, 128, 0, E), lhsT=ltri_sb[:], rhs=rt4,
                                 start=True, stop=True)
            wait_tok(SC, srd_tok)
            srk_tok = stage("A", 128, E, rank_mm, SC, srank_sb[:], s_stgAf)
            rnk_rd = (s_stgAr, s_stgAr.n)
            VE.wait_ge(s_bas[par].h, s_bas[par].n)
            wait_tok(VE, srk_tok)
            s_rte.inc(VE.tensor_tensor(out=rt0, in0=srank_sb[:],
                                       in1=basebc[:, par * E:(par + 1) * E], op=ALU.add))
            bas_rd[par] = s_rte.tok()
            srd_tok = s_rte.tok()
            s_rte.inc(VE.tensor_tensor_scan(out=rt1, data0=onesE_sb[:], data1=rt4,
                                            initial=0.0, op0=ALU.mult, op1=ALU.add))
            VE.wait_ge(s_rte.h, s_rte.n)
            s_rte.inc(VE.tensor_tensor(out=rt0, in0=rt0, in1=rt4, op=ALU.mult))
            for j in range(3):
                VE.wait_ge(s_rte.h, s_rte.n)
                s_rte.inc(VE.tensor_scalar(out=rt2, in0=rt1, scalar1=float(j + 1),
                                           scalar2=None, op0=ALU.is_equal))
                VE.wait_ge(s_rte.h, s_rte.n)
                s_rte.inc(VE.tensor_tensor(out=rt2, in0=rt2, in1=rt4, op=ALU.mult))
                VE.wait_ge(s_rte.h, s_rte.n)
                s_rte.inc(VE.tensor_tensor(out=rt3, in0=rt0, in1=rt2, op=ALU.mult))
                s_rte.inc(VE.reduce_sum(out=pf2_col, in_=rt2, axis=AX.X))
                VE.wait_ge(s_rte.h, s_rte.n)
                s_rte.inc(VE.reduce_sum(out=pf_col, in_=rt3, axis=AX.X))
                s_rte.inc(VE.scalar_tensor_tensor(out=pf3_col, in0=pf2_col, scalar=1.0,
                                                  in1=fc_sb[:, 3:4], op0=ALU.is_lt, op1=ALU.mult))
                VE.wait_ge(s_rte.h, s_rte.n)
                s_rte.inc(VE.tensor_tensor(out=pf_col, in0=pf_col, in1=pf3_col, op=ALU.add))
                VE.wait_ge(s_rte.h, s_rte.n)
                s_rte.inc(VE.tensor_copy(out=ptr_u[:, c0 + j:c0 + j + 1], in_=pf_col))
                s_rte.inc(VE.tensor_tensor(out=rt3, in0=gt_, in1=rt2, op=ALU.mult))
                VE.wait_ge(s_rte.h, s_rte.n)
                s_rte.inc(VE.reduce_sum(out=gval_sb[:, c0 + j:c0 + j + 1], in_=rt3, axis=AX.X))
            PTR_V[tile] = s_rte.n
            ffn_state["rnk_rd"] = rnk_rd
            ffn_state["srd_tok"] = srd_tok
            pump(3)
        GP.wait_ge(s_rte.h, PTR_V[tb_n - 1])
        wait_tok(GP, accp[tb_n - 1])
        GP.wait_ge(s_zf.h, ZF_ALL)
        if b > 0:
            wait_tok(GP, ffn_state["xgl_all"])
        for tile in range(tb_n):
            c0 = pcol0 + tile * 3
            for j in range(3):
                ist = GP.indirect_dma_start(
                    out=xg_h, out_offset=IndirectOffsetOnAxis(ptr_u[:, c0 + j:c0 + j + 1], 0),
                    in_=acc[:, tile * D:(tile + 1) * D], in_offset=None)
                s_xsc.inc(ist, 16)
        pump(6)
        XSC_ALL = s_xsc.n

        trn_tok = None
        w1ph_tok = None

        def load_xg(e_, dep_tok):
            SY.wait_ge(s_xsc.h, XSC_ALL)
            wait_tok(SY, dep_tok)
            for si in range(spc):
                dma(SY, x1tg[0][:, si * D:(si + 1) * D],
                    xg_h[e_ * CAP + si * 128:e_ * CAP + (si + 1) * 128, :], s_xgl)
            return s_xgl.n

        def t_round(si, first, xglv, war_tok):
            if first:
                wait_tok(PE, war_tok)
                PE.wait_ge(s_xgl.h, xglv)
            PE.wait_ge(s_psbf.h, s_psbf.n)
            last = None
            for dc in range(dc_n):
                last = PE.transpose(out=PSB[:, dc * 128:(dc + 1) * 128],
                                    in_=x1tg[0][:, si * D + dc * 128:si * D + (dc + 1) * 128],
                                    identity=idnb[:])
            s_psbr.inc(last)
            SC.wait_ge(s_psbr.h, s_psbr.n)
            if first:
                wait_tok(SC, war_tok)
            last = None
            for dc in range(dc_n):
                last = SC.copy(out=x1tg[1][:, dc * CAP + si * 128:dc * CAP + (si + 1) * 128],
                               in_=PSB[:, dc * 128:(dc + 1) * 128])
            s_psbf.inc(last)

        xgl_v = load_xg(0, x1tg_free[0])
        for si in range(spc):
            t_round(si, si == 0, xgl_v, x1tg_free[1])
        trn_tok = (s_psbr, s_psbr.n)
        TRNC_V = s_psbf.n
        if E > 1:
            xgl_v = load_xg(1, trn_tok)
        for g in range(hcg_n):
            w1_plan.append((0, g))
            w1_free_tok.append(None)
        for e in range(E):
            wcyc = b * E + e
            if e + 1 < E:
                for g in range(hcg_n):
                    w1_plan.append((e + 1, g))
                    w1_free_tok.append(None)
            pump_w1()
            if wcyc >= 1:
                wait_tok(SC, hs_free_tok[wcyc - 1])
            for g in range(hcg_n):
                dma(SC, w2sb[:, g * hcg * D:(g + 1) * hcg * D], w2s_h[e, g], s_w2)
            W2_V = s_w2.n
            pump(2)

            for hc in range(hc_n):
                g, hcl = hc // hcg, hc % hcg
                sl = hc % 2
                if hc == 0:
                    PE.wait_ge(s_psbf.h, TRNC_V)
                if sl == 1 and e == 0 and hc == 1:
                    wait_tok(PE, stg_free["B"])
                if hcl == 0:
                    wait_tok(PE, w1_load_tok[wcyc * hcg_n + g])
                k1 = len(ps1_free_at)
                if k1 >= 2:
                    PE.wait_ge(s_hs.h, ps1_free_at[k1 - 2])
                last = None
                for dc in range(dc_n):
                    o = hcl * dc_n * 128 + dc * 128
                    last = PE.matmul(out=ps1(sl, CAP),
                                     lhsT=w1buf[(wcyc * hcg_n + g) % 2][:, o:o + 128],
                                     rhs=x1tg[1][:, dc * CAP:dc * CAP + CAP],
                                     start=(dc == 0), stop=(dc == dc_n - 1))
                s_ps1r.inc(last)
                if hcl == hcg - 1:
                    w1_free_tok[wcyc * hcg_n + g] = (s_ps1r, s_ps1r.n)
                    pump_w1()
                SC.wait_ge(s_ps1r.h, s_ps1r.n)
                if hc == 0 and wcyc >= 1:
                    wait_tok(SC, hs_free_tok[wcyc - 1])
                if hc == 0 and wcyc == E and ffn_state["fadd_last"] is not None:
                    wait_tok(SC, ffn_state["fadd_last"])
                ist = SC.activation(out=hs[:, hc * CAP:(hc + 1) * CAP], in_=ps1(sl, CAP),
                                    func=AF.Relu, bias=b1_sb[:, hc * E + e:hc * E + e + 1])
                ps1_free_at.append(s_hs.inc(ist))
                if hc % 2 == 1:
                    pump(2)
            HS_V = s_hs.n
            w1ph_tok = (s_ps1r, s_ps1r.n)

            PE.wait_ge(s_w2.h, W2_V)
            for tb in range(spc):
                sl2 = tb % 2
                wait_tok(PE, ps2_free[sl2])
                last = None
                for hc in range(hc_n):
                    if tb == 0:
                        PE.wait_ge(s_hs.h, HS_V - hc_n + hc + 1)
                    for half in range(dhalf_n):
                        last = PE.matmul(out=PSF[:, sl2 * 1024 + half * 512:sl2 * 1024 + half * 512 + dhw],
                                         lhsT=hs[:, hc * CAP + tb * 128:hc * CAP + (tb + 1) * 128],
                                         rhs=w2sb[:, hc * D + half * 512:hc * D + half * 512 + dhw],
                                         start=(hc == 0), stop=(hc == hc_n - 1))
                s_ps2r.inc(last)
                if tb == spc - 1:
                    hs_free_tok.append((s_ps2r, s_ps2r.n))
                SC.wait_ge(s_ps2r.h, s_ps2r.n)
                wait_tok(SC, ogw_tok[0])
                ist = SC.copy(out=ogst[:], in_=PSF[:, sl2 * 1024:sl2 * 1024 + D])
                s_ogc.inc(ist)
                ps2_free[sl2] = s_ogc.tok()
                SY.wait_ge(s_ogc.h, s_ogc.n)
                dma(SY, og_h[e * CAP + tb * 128:e * CAP + (tb + 1) * 128, :],
                    ogst[:], s_ogw[0])
                ogw_tok[0] = s_ogw[0].tok()
                if e + 1 < E:
                    t_round(tb, tb == 0, xgl_v, w1ph_tok)
                pump(2)
            if e + 1 < E:
                trn_tok = (s_psbr, s_psbr.n)
                TRNC_V = s_psbf.n
                if e + 2 < E:
                    xgl_v = load_xg(e + 2, trn_tok)
        ffn_state["xgl_all"] = (s_xgl, s_xgl.n)
        x1tg_free[0] = trn_tok
        x1tg_free[1] = w1ph_tok

        OGW_V = (s_ogw[0].n, s_ogw[1].n)
        HSR_TOK = hs_free_tok[-1]
        fadd_tok = [None] * tb_n
        for tile in range(tb_n):
            par = tile % 2
            c0 = pcol0 + tile * 3
            at = acc[:, tile * D:(tile + 1) * D]
            if tile <= 1:
                GP.wait_ge(s_ogw[0].h, OGW_V[0])
                GP.wait_ge(s_ogw[1].h, OGW_V[1])
                GP.wait_ge(s_zf.h, ZF_ALL)
                wait_tok(GP, HSR_TOK)
            GP.wait_ge(s_rte.h, PTR_V[tile])
            if tile >= 2:
                wait_tok(GP, fadd_tok[tile - 2])
            for j in range(3):
                ist = GP.indirect_dma_start(
                    out=hs[:, (par * 3 + j) * D:(par * 3 + j + 1) * D], out_offset=None,
                    in_=og_h, in_offset=IndirectOffsetOnAxis(ptr_u[:, c0 + j:c0 + j + 1], 0))
                s_og3[par].inc(ist, 16)
            OG3_V = s_og3[par].n
            if tile == 0:
                wait_tok(PE, GT_TOK)
            wait_tok(PE, ps2_free[par])
            last = None
            for half in range(dhalf_n):
                last = PE.matmul(out=PSF[:, par * 1024 + half * 512:par * 1024 + half * 512 + dhw],
                                 lhsT=gT_sb[gp_][:, tile * 128:(tile + 1) * 128],
                                 rhs=b2_sb[:, half * 512:half * 512 + dhw],
                                 start=True, stop=True)
            s_ps2r.inc(last)
            B2_V = s_ps2r.n
            VE.wait_ge(s_ps2r.h, B2_V)
            if tile == 0:
                VE.wait_ge(s_xsc.h, XSC_ALL)
            s_vch.inc(VE.tensor_tensor(out=at, in0=at, in1=PSF[:, par * 1024:par * 1024 + D],
                                       op=ALU.add))
            ps2_free[par] = s_vch.tok()
            VE.wait_ge(s_og3[par].h, OG3_V)
            for j in range(3):
                VE.wait_ge(s_vch.h, s_vch.n)
                ist = VE.scalar_tensor_tensor(out=at, in0=hs[:, (par * 3 + j) * D:(par * 3 + j + 1) * D],
                                              scalar=gval_sb[:, c0 + j:c0 + j + 1], in1=at,
                                              op0=ALU.mult, op1=ALU.add)
                if j == 2:
                    s_accu.inc(ist)
                else:
                    s_vch.inc(ist)
            fadd_tok[tile] = (s_accu, s_accu.n)
            ln2t = hs[:, 6 * D:7 * D]
            VE.wait_ge(s_accu.h, s_accu.n)
            s_vch.inc(VE.reduce_sum(out=c_sum, in_=at, axis=AX.X))
            s_vch.inc(VE.tensor_tensor(out=ln2t, in0=at, in1=at, op=ALU.mult))
            VE.wait_ge(s_vch.h, s_vch.n)
            s_vch.inc(VE.reduce_sum(out=c_sq, in_=ln2t, axis=AX.X))
            VE.wait_ge(s_vch.h, s_vch.n)
            s_vch.inc(VE.tensor_scalar(out=c_mean, in0=c_sum, scalar1=1.0 / D, scalar2=None, op0=ALU.mult))
            s_vch.inc(VE.tensor_scalar(out=c_var, in0=c_sq, scalar1=1.0 / D, scalar2=None, op0=ALU.mult))
            VE.wait_ge(s_vch.h, s_vch.n)
            s_vch.inc(VE.tensor_tensor(out=c_m2, in0=c_mean, in1=c_mean, op=ALU.mult))
            VE.wait_ge(s_vch.h, s_vch.n)
            s_ln2v.inc(VE.tensor_tensor(out=c_var, in0=c_var, in1=c_m2, op=ALU.subtract))
            SC.wait_ge(s_ln2v.h, s_ln2v.n)
            s_sch.inc(SC.activation(out=c_lnv, in_=c_var, func=AF.Ln, bias=1e-5))
            SC.wait_ge(s_sch.h, s_sch.n)
            s_ln2s.inc(SC.activation(out=c_rstd, in_=c_lnv, func=AF.Exp, scale=-0.5))
            VE.wait_ge(s_ln2s.h, s_ln2s.n)
            if s_outv.n >= 1:
                VE.wait_ge(s_st[0].h, s_st[0].n)
            s_outv.inc(VE.tensor_scalar(out=out_sb[0][:], in0=at, scalar1=c_mean,
                                        scalar2=c_rstd, op0=ALU.subtract, op1=ALU.mult))
            SY.wait_ge(s_outv.h, s_outv.n)
            dma(SY, out_h[b, tile * 128:(tile + 1) * 128, :], out_sb[0][:], s_st[0])
            pump(2)
        ffn_state["fadd_last"] = (s_outv, s_outv.n)

    def drain(g):
        for _ in g:
            pass

    def pump_n(g, k):
        for _ in range(k):
            if next(g, 'DONE') == 'DONE':
                return False
        return True

    p0 = p0_gen()
    m0 = mamba_gen(0)
    alive_p, alive_m = True, True
    while alive_p or alive_m:
        if alive_m:
            alive_m = pump_n(m0, 12)
        if alive_p:
            alive_p = pump_n(p0, 1)
    P0W2_ALL = s_p0w2.n
    m1 = mamba_gen(1) if B > 1 else None
    def pump1(k=1):
        if m1 is not None:
            pump_n(m1, k)
    pump1(48)
    ffn_emit(0, pump1)
    if B > 1:
        drain(m1)
        ffn_emit(1, lambda k=1: None)

    SY.wait_ge(s_st[0].h, s_st[0].n)
    SY.wait_ge(s_st[1].h, s_st[1].n)
    ctx.close()
    return nc


def aux_inputs(cfg):
    D, E, tb_n, CAP = cfg.D, cfg.E, cfg.tb_n, cfg.CAP
    fc = np.zeros((128, 8), np.float32)
    fc[:, 0] = 1e-5
    fc[:, 1] = -0.5
    fc[:, 2] = np.float32(1.0) / np.float32(D)
    fc[:, 3] = np.float32(E * CAP)
    lt = np.triu(np.ones((tb_n + 1, tb_n), np.float32), 1)
    lt[tb_n, :] = 1.0
    return {
        "aux_fconst": fc,
        "aux_idn32": np.eye(128, dtype=np.float32),
        "aux_idnb": np.eye(128).astype(ml_dtypes.bfloat16),
        "aux_onesc": np.ones((128, 1), np.float32),
        "aux_onescb": np.ones((128, 1)).astype(ml_dtypes.bfloat16),
        "aux_ltri": np.triu(np.ones((128, 128), np.float32), 1),
        "aux_lt": lt,
        "aux_ecap": (np.arange(E, dtype=np.float32) * CAP).reshape(1, E),
        "aux_onesE": np.ones((128, E), np.float32),
        "aux_zrow": np.zeros((128, D), ml_dtypes.bfloat16),
    }


def make_in_map(cfg, inputs, xslice):
    m = {}
    m["L0_attn_sequence"] = np.ascontiguousarray(xslice)
    for k in ("L0_attn_A", "L0_attn_W_B", "L0_attn_W_C", "L0_attn_W_Delta",
              "L0_ffn_gate", "L0_ffn_W1", "L0_ffn_b1", "L0_ffn_W2", "L0_ffn_b2"):
        m[k] = np.ascontiguousarray(inputs[k])
    m["L0_attn_b_B"] = np.ascontiguousarray(inputs["L0_attn_b_B"]).reshape(cfg.N, 1)
    m["L0_attn_b_C"] = np.ascontiguousarray(inputs["L0_attn_b_C"]).reshape(cfg.N, 1)
    m["L0_attn_b_Delta"] = np.ascontiguousarray(inputs["L0_attn_b_Delta"]).reshape(1, 1)
    m["L0_attn_bias_Delta"] = np.ascontiguousarray(inputs["L0_attn_bias_Delta"]).reshape(1, 1)
    m.update(aux_inputs(cfg))
    return m


def kernel(**inputs):
    from concourse.bass_utils import run_bass_kernel_spmd
    cfg = CFG(B=2, T=2048, D=1024, H=2048)
    nc = bass.Bass("TRN2", target_bir_lowering=False)
    build(nc, cfg)
    x = inputs["L0_attn_sequence"]
    in_maps = [make_in_map(cfg, inputs, x[2 * c:2 * c + 2]) for c in range(8)]
    res = run_bass_kernel_spmd(nc, in_maps, core_ids=list(range(8)))
    return np.concatenate([r["out"] for r in res.results], axis=0)


# version 19 (final)
# speedup vs baseline: 1.1521x; 1.0046x over previous
import numpy as np
import ml_dtypes
from contextlib import ExitStack

import concourse.bass as bass
import concourse.mybir as mybir

F32 = mybir.dt.float32
BF16 = mybir.dt.bfloat16
AF = mybir.ActivationFunctionType
ALU = mybir.AluOpType
AX = mybir.AxisListType


class CFG:
    def __init__(self, B=2, T=2048, D=1024, H=2048):
        self.B, self.T, self.D, self.H = B, T, D, H
        self.E = 16
        self.N = 16
        self.TC = 512
        assert T % self.TC == 0 and D % 128 == 0 and H % 128 == 0
        self.tc_n = T // self.TC
        self.dc_n = D // 128
        self.hc_n = H // 128
        self.CAP = 512 if T == 2048 else T // 4 + 128
        self.spc = self.CAP // 128
        self.hcg = min(2, self.hc_n)
        assert self.hc_n % self.hcg == 0
        self.hcg_n = self.hc_n // self.hcg
        self.tb_n = T // 128
        self.dhalf_n = max(1, D // 512)
        self.dhw = min(512, D)


class Sem:
    __slots__ = ("h", "n", "name")

    def __init__(self, ctx, nc, name):
        self.h = ctx.enter_context(nc.semaphore(name))
        self.n = 0
        self.name = name

    def inc(self, instr, k=1):
        instr.then_inc(self.h, k)
        self.n += k
        return self.n

    def tok(self):
        return (self, self.n)


def wait_tok(eng, tok):
    if tok is not None:
        eng.wait_ge(tok[0].h, tok[1])


def build(nc, cfg: CFG):
    B, T, D, H, E, N = cfg.B, cfg.T, cfg.D, cfg.H, cfg.E, cfg.N
    TC, tc_n, dc_n, hc_n, hcg, hcg_n, tb_n = (
        cfg.TC, cfg.tc_n, cfg.dc_n, cfg.hc_n, cfg.hcg, cfg.hcg_n, cfg.tb_n)
    tg_n = tc_n
    tpc = TC // 128
    dhalf_n, dhw = cfg.dhalf_n, cfg.dhw
    CAP, spc = cfg.CAP, cfg.spc
    from concourse.bass import IndirectOffsetOnAxis

    x_h = nc.dram_tensor("L0_attn_sequence", [B, T, D], F32, kind="ExternalInput").ap()
    A_h = nc.dram_tensor("L0_attn_A", [D, N], F32, kind="ExternalInput").ap()
    WB_h = nc.dram_tensor("L0_attn_W_B", [N, D], F32, kind="ExternalInput").ap()
    bB_h = nc.dram_tensor("L0_attn_b_B", [N, 1], F32, kind="ExternalInput").ap()
    WC_h = nc.dram_tensor("L0_attn_W_C", [N, D], F32, kind="ExternalInput").ap()
    bC_h = nc.dram_tensor("L0_attn_b_C", [N, 1], F32, kind="ExternalInput").ap()
    WD_h = nc.dram_tensor("L0_attn_W_Delta", [1, D], F32, kind="ExternalInput").ap()
    bD_h = nc.dram_tensor("L0_attn_b_Delta", [1, 1], F32, kind="ExternalInput").ap()
    biasD_h = nc.dram_tensor("L0_attn_bias_Delta", [1, 1], F32, kind="ExternalInput").ap()
    gate_h = nc.dram_tensor("L0_ffn_gate", [D, E], F32, kind="ExternalInput").ap()
    W1_h = nc.dram_tensor("L0_ffn_W1", [E, D, H], F32, kind="ExternalInput").ap()
    b1_h = nc.dram_tensor("L0_ffn_b1", [E, H], F32, kind="ExternalInput").ap()
    W2_h = nc.dram_tensor("L0_ffn_W2", [E, H, D], F32, kind="ExternalInput").ap()
    b2_h = nc.dram_tensor("L0_ffn_b2", [E, D], F32, kind="ExternalInput").ap()
    idn32_h = nc.dram_tensor("aux_idn32", [128, 128], F32, kind="ExternalInput").ap()
    idnb_h = nc.dram_tensor("aux_idnb", [128, 128], BF16, kind="ExternalInput").ap()
    onesc_h = nc.dram_tensor("aux_onesc", [128, 1], F32, kind="ExternalInput").ap()
    onescb_h = nc.dram_tensor("aux_onescb", [128, 1], BF16, kind="ExternalInput").ap()
    fconst_h = nc.dram_tensor("aux_fconst", [128, 8], F32, kind="ExternalInput").ap()
    ltri_h = nc.dram_tensor("aux_ltri", [128, 128], F32, kind="ExternalInput").ap()
    lt_h = nc.dram_tensor("aux_lt", [cfg.tb_n + 1, cfg.tb_n], F32, kind="ExternalInput").ap()
    ecap_h = nc.dram_tensor("aux_ecap", [1, E], F32, kind="ExternalInput").ap()
    onesE_h = nc.dram_tensor("aux_onesE", [128, E], F32, kind="ExternalInput").ap()
    zrow_h = nc.dram_tensor("aux_zrow", [128, D], BF16, kind="ExternalInput").ap()

    out_h = nc.dram_tensor("out", [B, T, D], F32, kind="ExternalOutput").ap()

    w1s_h = nc.dram_tensor("w1s", [E, hcg_n, 128, hcg * dc_n * 128], BF16, kind="Internal").ap()
    w2s_h = nc.dram_tensor("w2s", [E, hcg_n, 128, hcg * D], BF16, kind="Internal").ap()
    x1d_h = nc.dram_tensor("x1d", [B, dc_n, 128, T], BF16, kind="Internal").ap()
    rows_h = nc.dram_tensor("rowsb", [2 * N + 1, TC], F32, kind="Internal").ap()
    rows16_h = nc.dram_tensor("rowsb16", [2 * N, TC], BF16, kind="Internal").ap()
    lnr_h = nc.dram_tensor("lnrb", [2, TC], F32, kind="Internal").ap()
    xg_h = nc.dram_tensor("xg", [E * CAP + 3 * tb_n, D], BF16, kind="Internal").ap()
    og_h = nc.dram_tensor("og", [E * CAP + 3 * tb_n, D], BF16, kind="Internal").ap()
    cntd_h = nc.dram_tensor("cntd", [tb_n, E], F32, kind="Internal").ap()
    based_h = nc.dram_tensor("based", [tb_n, E], F32, kind="Internal").ap()

    ctx = ExitStack()
    e_ = ctx.enter_context

    idn32 = e_(nc.sbuf_tensor("idn32", [128, 128], F32))
    idnb = e_(nc.sbuf_tensor("idnb", [128, 128], BF16))
    onesc = e_(nc.sbuf_tensor("onesc", [128, 1], F32))
    onescb = e_(nc.sbuf_tensor("onescb", [128, 1], BF16))
    A_sb = e_(nc.sbuf_tensor("A_sb", [128, dc_n * N], F32))
    invA_sb = e_(nc.sbuf_tensor("invA_sb", [128, dc_n * N], F32))
    gate_sb = e_(nc.sbuf_tensor("gate_sb", [128, dc_n * E], F32))
    wbcd_sb = e_(nc.sbuf_tensor("wbcd_sb", [128, dc_n * 33], F32))
    bias33 = e_(nc.sbuf_tensor("bias33", [33, 1], F32))
    biasd_tmp = e_(nc.sbuf_tensor("biasd_tmp", [33, 1], F32))
    b1_sb = e_(nc.sbuf_tensor("b1_sb", [128, hc_n * E], F32))
    b2_sb = e_(nc.sbuf_tensor("b2_sb", [E, D], BF16))
    fc_sb = e_(nc.sbuf_tensor("fc_sb", [128, 8], F32))
    ltri_sb = e_(nc.sbuf_tensor("ltri_sb", [128, 128], F32))
    lt_sb = e_(nc.sbuf_tensor("lt_sb", [tb_n + 1, tb_n], F32))
    onesE_sb = e_(nc.sbuf_tensor("onesE_sb", [128, E], F32))
    cntr_sb = e_(nc.sbuf_tensor("cntr_sb", [tb_n + 1, E], F32))
    nc.const_aps.register(F32, 1e-5, fc_sb[:, 0:1])
    nc.const_aps.register(F32, -0.5, fc_sb[:, 1:2])
    nc.const_aps.register(F32, 1.0 / D, fc_sb[:, 2:3])

    PSF = e_(nc.psum_tensor("PSF", [128, 3584], F32))
    PSB = e_(nc.psum_tensor("PSB", [128, 1024], BF16))

    def ps1(slot, w=None):
        w = TC if w is None else w
        return PSF[:, 2048 + slot * 512:2048 + slot * 512 + w]

    def stgA(p0, p1, c0, c1):
        return PSF[p0:p1, 3072 + c0:3072 + c1]

    def stgB(p0, p1, c0, c1):
        return PSF[p0:p1, 2560 + c0:2560 + c1]

    def S(name):
        return Sem(ctx, nc, name)

    s_ld = S("ld"); s_ldg = S("ldg"); s_tab = S("tab")
    s_p0l = S("p0l"); s_p0c = S("p0c")
    s_p0w = [S("p0wa"), S("p0wb")]; s_p0w2 = S("p0w2")
    s_xn = [S("xn0"), S("xn1")]
    s_stgAr = S("stgAr"); s_stgAf = S("stgAf")
    s_stgBr = S("stgBr"); s_stgBf = S("stgBf")
    s_bcd = S("bcd"); s_row = S("row")
    s_dd = S("dd"); s_bm = [S("bm0"), S("bm1")]; s_cm = [S("cm0"), S("cm1")]; s_lnw = S("lnw")
    s_row16 = S("row16")
    s_ab = S("ab"); s_bx = S("bx"); s_in1 = S("in1"); s_scan = S("scan"); s_icm = S("icm")
    s_carry = S("carry"); s_yt = S("yt"); s_yadd = S("yadd"); s_yinit = S("yinit")
    s_sq = S("sq"); s_sqmm = S("sqmm"); s_lnr = S("lnr")
    s_x1n = S("x1n"); s_x1 = S("x1"); s_x1d = [S("x1d0"), S("x1d1")]; s_x1l = [S("x1l0"), S("x1l1")]
    s_gv = S("gv"); s_gs = S("gs"); s_g = S("g")
    s_w1 = [S("w1a"), S("w1b")]; s_w2 = S("w2")
    s_ps1r = S("ps1r"); s_hs = S("hs")
    s_ps2r = S("ps2r"); s_accu = S("accu"); s_acci = S("acci")
    s_psbr = S("psbr"); s_psbf = S("psbf")
    s_ln2v = S("ln2v"); s_ln2s = S("ln2s"); s_outv = S("outv")
    s_vch = S("vch"); s_sch = S("sch"); s_gch = S("gch")
    s_st = [S("st0"), S("st1")]
    s_rte = S("rte"); s_cb = S("cb"); s_bas = [S("bas0"), S("bas1")]
    s_xsc = S("xsc"); s_xgl = S("xgl"); s_zf = S("zf")
    s_ogc = S("ogc"); s_ogw = [S("ogw0"), S("ogw1")]
    s_og3 = [S("og30"), S("og31")]

    SY, SC, VE, PE, GP = nc.sync, nc.scalar, nc.vector, nc.tensor, nc.gpsimd

    blk = e_(nc.Block())

    def dma(eng, out, in_, sem):
        eng.dma_start(out=out, in_=in_).then_inc(sem.h, 16)
        sem.n += 16
        return sem.n

    stg_free = {"A": None, "B": None}

    def stage(bank, m_rows, m_cols, pe_emit, ev_eng, dst, ev_sem):
        reg, rs = (stgA, s_stgAr) if bank == "A" else (stgB, s_stgBr)
        wait_tok(PE, stg_free[bank])
        rs.inc(pe_emit())
        ev_eng.wait_ge(rs.h, rs.n)
        ist = ev_eng.copy(out=dst, in_=reg(0, m_rows, 0, m_cols))
        ev_sem.inc(ist)
        stg_free[bank] = ev_sem.tok()
        return ev_sem.tok()

    tabctx = ExitStack()
    wbt = tabctx.enter_context(nc.sbuf_tensor("wbt", [N, D], F32))
    wct = tabctx.enter_context(nc.sbuf_tensor("wct", [N, D], F32))
    wdt = tabctx.enter_context(nc.sbuf_tensor("wdt", [1, D], F32))
    b1t = tabctx.enter_context(nc.sbuf_tensor("b1t", [E, H], F32))
    dma(SY, idn32[:], idn32_h, s_ld)
    dma(SY, idnb[:], idnb_h, s_ld)
    dma(SY, onesc[:], onesc_h, s_ld)
    dma(SY, onescb[:], onescb_h, s_ld)
    dma(SY, fc_sb[:], fconst_h, s_ld)
    for dc in range(dc_n):
        dma(SY, A_sb[:, dc * N:(dc + 1) * N], A_h[dc * 128:(dc + 1) * 128, :], s_ld)
        dma(SY, gate_sb[:, dc * E:(dc + 1) * E], gate_h[dc * 128:(dc + 1) * 128, :], s_ld)
    dma(SY, wbt[:], WB_h, s_ld)
    dma(SY, wct[:], WC_h, s_ld)
    dma(SY, wdt[:], WD_h, s_ld)
    dma(SY, b1t[:], b1_h, s_ld)
    dma(GP, b2_sb[:], b2_h, s_ldg)
    dma(SY, ltri_sb[:], ltri_h, s_ld)
    dma(SY, lt_sb[:], lt_h, s_ld)
    dma(SY, onesE_sb[:], onesE_h, s_ld)
    dma(SY, cntr_sb[tb_n:tb_n + 1, :], ecap_h, s_ld)
    for i in range(E * CAP // 128):
        dma(GP, xg_h[i * 128:(i + 1) * 128, :], zrow_h, s_zf)
    dma(GP, og_h[E * CAP:E * CAP + 3 * tb_n, :], zrow_h[0:3 * tb_n, :], s_zf)
    ZF_ALL = s_zf.n
    dma(SY, bias33[0:N, 0:1], bB_h, s_ld)
    dma(SY, bias33[N:2 * N, 0:1], bC_h, s_ld)
    dma(SY, bias33[32:33, 0:1], bD_h, s_ld)
    dma(SY, biasd_tmp[32:33, 0:1], biasD_h, s_ld)
    LD_ALL = s_ld.n

    VE.wait_ge(s_ld.h, LD_ALL)
    VE.reciprocal(out=invA_sb[:], in_=A_sb[:])
    s_tab.inc(VE.tensor_tensor(out=bias33[32:33, 0:1], in0=bias33[32:33, 0:1],
                               in1=biasd_tmp[32:33, 0:1], op=ALU.add))

    PE.wait_ge(s_ld.h, LD_ALL)
    tab_jobs = []
    for dc in range(dc_n):
        slc = slice(dc * 128, (dc + 1) * 128)
        tab_jobs.append((wbt[:, slc], N, wbcd_sb[:, dc * 33:dc * 33 + N]))
        tab_jobs.append((wct[:, slc], N, wbcd_sb[:, dc * 33 + N:dc * 33 + 2 * N]))
        tab_jobs.append((wdt[:, slc], 1, wbcd_sb[:, dc * 33 + 32:dc * 33 + 33]))
    for hc in range(hc_n):
        tab_jobs.append((b1t[:, hc * 128:(hc + 1) * 128], E, b1_sb[:, hc * E:(hc + 1) * E]))
    for src, m, dst in tab_jobs:
        kk = src.shape[0]
        stage("A", 128, m,
              lambda src=src, m=m, kk=kk: PE.transpose(out=stgA(0, 128, 0, m), in_=src,
                                                       identity=idn32[0:kk, 0:kk]),
              SC, dst, s_stgAf)
    s_tab.inc(SC.copy(out=biasd_tmp[0:1, 0:1], in_=bias33[0:1, 0:1]))
    TAB_ALL = s_tab.n
    tabctx.close()

    def p0_gen():
        for e in range(E):
            for g in range(hcg_n):
                src = W2_h[e, g * hcg * 128:(g + 1) * hcg * 128, :]
                src = src.rearrange("(hcl p) j -> p hcl j", p=128)
                dst = w2s_h[e, g].rearrange("p (hcl j) -> p hcl j", hcl=hcg)
                dma(GP, dst, src, s_p0w2)
            yield
        w1tmp = [w2sb[:, dc * H:(dc + 1) * H] for dc in range(dc_n)]
        asmw = hcg * dc_n * 128
        asm1 = [hs[:, i * asmw:(i + 1) * asmw] for i in range(2)]
        cpe = hc_n * dc_n
        GP.wait_ge(s_tab.h, TAB_ALL)
        for e in range(E):
            for dc in range(dc_n):
                if e > 0:
                    GP.wait_ge(s_p0c.h, e * cpe)
                dma(GP, w1tmp[dc], W1_h[e, dc * 128:(dc + 1) * 128, :], s_p0l)
            yield
            for g in range(hcg_n):
                gi = e * hcg_n + g
                a = asm1[gi % 2]
                VE.wait_ge(s_p0l.h, s_p0l.n)
                if gi >= 2:
                    VE.wait_ge(s_p0w[gi % 2].h, s_p0w[gi % 2].n)
                last = None
                for hcl in range(hcg):
                    hc = g * hcg + hcl
                    for dc in range(dc_n):
                        o = hcl * dc_n * 128 + dc * 128
                        last = VE.tensor_copy(out=a[:, o:o + 128],
                                              in_=w1tmp[dc][:, hc * 128:(hc + 1) * 128])
                s_p0c.inc(last, hcg * dc_n)
                SY.wait_ge(s_p0c.h, s_p0c.n)
                dma(SY, w1s_h[e, g], a, s_p0w[gi % 2])
                if g % 2 == 1:
                    yield

    xn = [e_(nc.sbuf_tensor(f"xn{i}", [128, D], F32)) for i in range(1)]
    xc = [e_(nc.sbuf_tensor(f"xc{i}", [128, TC], F32)) for i in range(dc_n)]
    y_sb = [e_(nc.sbuf_tensor(f"y{i}", [128, TC], F32)) for i in range(dc_n)]
    bcd_sb = e_(nc.sbuf_tensor("bcd_sb", [33, TC], F32))
    sp_tmp = e_(nc.sbuf_tensor("sp_tmp", [33, TC], F32))
    drep = e_(nc.sbuf_tensor("drep", [128, TC], F32))
    bmrep = [e_(nc.sbuf_tensor(f"bmrep{i}", [128, TC], BF16)) for i in range(2)]
    cmrep = [e_(nc.sbuf_tensor(f"cmrep{i}", [128, TC], BF16)) for i in range(2)]
    icm_b = [e_(nc.sbuf_tensor(f"icmb{i}", [128, TC], BF16)) for i in range(2)]
    rrep32 = e_(nc.sbuf_tensor("rrep32", [128, TC], F32))
    ln1tmp = e_(nc.sbuf_tensor("ln1tmp", [128, TC], F32))
    ab_b = [e_(nc.sbuf_tensor(f"abb{i}", [128, TC], BF16)) for i in range(6)]
    bx_b = [e_(nc.sbuf_tensor(f"bxb{i}", [128, TC], BF16)) for i in range(4)]
    in1_b = [e_(nc.sbuf_tensor(f"in1b{i}", [128, TC], BF16)) for i in range(2)]
    sc_b = [e_(nc.sbuf_tensor(f"scb{i}", [128, TC], F32)) for i in range(2)]
    yt_b = [e_(nc.sbuf_tensor(f"ytb{i}", [128, TC], BF16)) for i in range(4)]
    sq_b = in1_b
    carry = e_(nc.sbuf_tensor("carry", [128, dc_n * N], F32))
    lnrow = e_(nc.sbuf_tensor("lnrow", [1, 2 * TC], F32))
    eg_sb = e_(nc.sbuf_tensor("eg_sb", [128, E], F32))
    m8_sb = e_(nc.sbuf_tensor("m8_sb", [128, 8], F32))
    mask_sb = e_(nc.sbuf_tensor("mask_sb", [128, E], F32))
    gsel_sb = e_(nc.sbuf_tensor("gsel_sb", [128, E], F32))
    cols = e_(nc.sbuf_tensor("cols", [128, 16], F32))
    g_sb = [e_(nc.sbuf_tensor(f"g_sb{i}", [128, tb_n * E], F32)) for i in range(2)]
    gT_sb = [e_(nc.sbuf_tensor(f"gT_sb{i}", [E, T], BF16)) for i in range(2)]
    x1stg = [e_(nc.sbuf_tensor(f"x1stg{i}", [128, TC], BF16)) for i in range(1)]
    x1tg = [e_(nc.sbuf_tensor(f"x1tg{i}", [128, dc_n * TC], BF16)) for i in range(2)]
    acc = e_(nc.sbuf_tensor("acc", [128, tb_n * D], BF16))
    hs = e_(nc.sbuf_tensor("hs", [128, max(hc_n * TC, 7 * D)], BF16))
    mskt_sb = e_(nc.sbuf_tensor("mskt_sb", [128, E], F32))
    srank_sb = e_(nc.sbuf_tensor("srank_sb", [128, E], F32))
    cnt_sb = e_(nc.sbuf_tensor("cnt_sb", [1, tb_n * E], F32))
    bs_sb = e_(nc.sbuf_tensor("bs_sb", [tb_n, E], F32))
    basebc = e_(nc.sbuf_tensor("basebc", [128, 2 * E], F32))
    rt_sb = e_(nc.sbuf_tensor("rt_sb", [128, 5 * E + 3], F32))
    ptr_u = e_(nc.sbuf_tensor("ptr_u", [128, 2 * tb_n * 3], mybir.dt.uint32))
    gval_sb = e_(nc.sbuf_tensor("gval_sb", [128, 2 * tb_n * 3], F32))
    ogst = e_(nc.sbuf_tensor("ogst", [128, D], BF16))
    w1buf = [e_(nc.sbuf_tensor(f"w1buf{i}", [128, hcg * dc_n * 128], BF16)) for i in range(2)]
    w2sb = e_(nc.sbuf_tensor("w2sb", [128, hc_n * D], BF16))
    out_sb = [e_(nc.sbuf_tensor(f"out_sb{i}", [128, D], F32)) for i in range(1)]

    c_rowmax = cols[:, 0:1]; c_negmax = cols[:, 1:2]
    c_den = cols[:, 2:3]; c_den2 = cols[:, 3:4]; c_rec = cols[:, 4:5]
    c_sum = cols[:, 5:6]; c_sq = cols[:, 6:7]; c_mean = cols[:, 7:8]
    c_m2 = cols[:, 8:9]; c_var = cols[:, 9:10]; c_lnv = cols[:, 10:11]; c_rstd = cols[:, 11:12]

    for ENG in (PE, VE, SC, GP, SY):
        ENG.wait_ge(s_tab.h, TAB_ALL)
        ENG.wait_ge(s_ldg.h, 16)

    carry_val = {}
    acc_last = {}
    w1_load_tok = []
    w1_free_tok = []
    w1_plan = []
    w1_state = {"emitted": 0}

    def pump_w1():
        while w1_state["emitted"] < len(w1_plan):
            k = w1_state["emitted"]
            if k >= 2 and w1_free_tok[k - 2] is None:
                return
            if k >= 2:
                wait_tok(SY, w1_free_tok[k - 2])
            e_, g_ = w1_plan[k]
            dma(SY, w1buf[k % 2][:], w1s_h[e_, g_], s_w1[k % 2])
            w1_load_tok.append((s_w1[k % 2], s_w1[k % 2].n))
            w1_state["emitted"] = k + 1
    ps1_free_at = []
    hs_free_tok = []
    xn_free_tok = []
    rep = {}
    gpe_token = None

    shared = {'gpe': None}
    mstate = {}
    ffn_state = {"x1tg_free": [None, None], "ps2_free": [None, None],
                 "ogw_tok": [None, None], "xgl_all": None, "fadd_last": None}
    def mamba_gen(b):
        gp_ = b % 2
        gpe_token = shared['gpe']
        for tc in range(tc_n):
            t0 = tc * TC
            for tt in range(tpc):
                kx = s_xn[0].n // 16
                if kx >= 1:
                    wait_tok(SY, xn_free_tok[kx - 1])
                slx = 0
                dma(SY, xn[slx][:], x_h[b, t0 + tt * 128:t0 + (tt + 1) * 128, :], s_xn[slx])
                PE.wait_ge(s_xn[slx].h, s_xn[slx].n)
                nuse = (dc_n + 3) // 4
                for u in range(nuse):
                    ndc = min(4, dc_n - u * 4)
                    bank = "A"
                    reg, rs = stgA, s_stgAr
                    fsem = s_stgAf
                    wait_tok(PE, stg_free[bank])
                    last = None
                    for i in range(ndc):
                        dc = u * 4 + i
                        last = PE.transpose(out=reg(0, 128, i * 128, (i + 1) * 128),
                                            in_=xn[slx][:, dc * 128:(dc + 1) * 128],
                                            identity=idn32[:])
                    rs.inc(last)
                    SC.wait_ge(rs.h, rs.n)
                    last2 = None
                    for i in range(ndc):
                        dc = u * 4 + i
                        last2 = SC.copy(out=xc[dc][:, tt * 128:(tt + 1) * 128],
                                        in_=reg(0, 128, i * 128, (i + 1) * 128))
                    fsem.inc(last2)
                    stg_free[bank] = fsem.tok()
                xn_free_tok.append(stg_free["A"])
                yield
            xc_tok = xn_free_tok[-1]

            wait_tok(PE, xc_tok)
            wait_tok(PE, stg_free["A"])
            last = None
            for dc in range(dc_n):
                last = PE.matmul(out=stgA(0, 33, 0, TC), lhsT=wbcd_sb[:, dc * 33:(dc + 1) * 33],
                                 rhs=xc[dc][:, 0:TC], start=(dc == 0), stop=(dc == dc_n - 1))
            s_stgAr.inc(last)
            VE.wait_ge(s_stgAr.h, s_stgAr.n)
            ist = VE.tensor_scalar(out=bcd_sb[:], in0=stgA(0, 33, 0, TC), scalar1=bias33[:],
                                   scalar2=None, op0=ALU.add)
            s_stgAf.inc(ist)
            stg_free["A"] = s_stgAf.tok()
            bcd_tok = s_stgAf.tok()
            z = bcd_sb[32:33, :]
            t1 = sp_tmp[32:33, 0:TC]
            wait_tok(SC, bcd_tok)
            s_sch.inc(SC.activation(out=t1, in_=z, func=AF.Abs))
            SC.wait_ge(s_sch.h, s_sch.n)
            s_sch.inc(SC.activation(out=t1, in_=t1, func=AF.Exp, scale=-1.0))
            SC.wait_ge(s_sch.h, s_sch.n)
            s_sch.inc(SC.activation(out=t1, in_=t1, func=AF.Ln, bias=1.0))
            SC.wait_ge(s_sch.h, s_sch.n)
            s_bcd.inc(SC.activation(out=z, in_=z, func=AF.Relu))
            VE.wait_ge(s_bcd.h, s_bcd.n)
            s_bcd.inc(VE.tensor_tensor(out=z, in0=z, in1=t1, op=ALU.add))
            SY.wait_ge(s_bcd.h, s_bcd.n)
            dma(SY, rows_h, bcd_sb[:], s_row)
            GP.wait_ge(s_bcd.h, s_bcd.n)
            dma(GP, rows16_h, bcd_sb[0:2 * N, :], s_row16)
            ROW_V = s_row.n
            ROW16_V = s_row16.n
            SY.wait_ge(s_row.h, ROW_V)
            SY.wait_ge(s_ab.h, s_ab.n)
            SY.wait_ge(s_x1n.h, s_x1n.n)
            dma(SY, drep[:], rows_h[32:33, :].to_broadcast((128, TC)), s_dd)
            DD_V = s_dd.n
            yield

            wait_tok(GP, xc_tok)
            if b > 0 or tc > 0:
                wait_tok(GP, gpe_token)
                GP.wait_ge(s_x1.h, s_x1.n)
            last = None
            for dc in range(dc_n):
                last = GP.tensor_copy(out=y_sb[dc][:], in_=xc[dc][:])
            s_yinit.inc(last)
            YINIT_V = s_yinit.n

            ylast = {db: None for db in range(dc_n)}
            iters = N * dc_n
            BMv = {}
            CMv = {}

            def emit_repdma(n):
                sl = n % 2
                SY.wait_ge(s_row16.h, ROW16_V)
                SY.wait_ge(s_row.h, ROW_V)
                if n >= 2:
                    SY.wait_ge(s_bx.h, BX_BASE0 + (n - 1) * dc_n)
                    SY.wait_ge(s_icm.h, ICM_BASE0 + (n - 1) * dc_n)
                dma(SY, bmrep[sl][:], rows16_h[n:n + 1, :].to_broadcast((128, TC)), s_bm[sl])
                BMv[n] = s_bm[sl].n
                dma(SY, cmrep[sl][:], rows16_h[N + n:N + n + 1, :].to_broadcast((128, TC)), s_cm[sl])
                CMv[n] = s_cm[sl].n

            BX_BASE0 = s_bx.n
            YT_BASE0 = s_yt.n
            SCAN_BASE0 = s_scan.n
            IN1_BASE0 = s_in1.n
            AB_BASE0 = s_ab.n
            YADD_BASE0 = s_yadd.n
            ICM_BASE0 = s_icm.n

            def fix_rep_wait(n):
                pass

            emit_repdma(0)
            if N > 1:
                emit_repdma(1)
            for k in range(iters + 2):
                if k < iters:
                    n, db = k // dc_n, k % dc_n
                    sl4 = k % 4
                    sl6 = k % 6
                    ci = db * N + n
                    SC.wait_ge(s_dd.h, DD_V)
                    if k >= 6:
                        SC.wait_ge(s_scan.h, SCAN_BASE0 + k - 5)
                    s_ab.inc(SC.activation(out=ab_b[sl6][:], in_=drep[:], func=AF.Exp,
                                           scale=A_sb[:, ci:ci + 1]))
                    SC.wait_ge(s_cm[n % 2].h, CMv[n])
                    if k >= 2:
                        SC.wait_ge(s_yt.h, YT_BASE0 + k - 1)
                    s_icm.inc(SC.activation(out=icm_b[k % 2][:], in_=cmrep[n % 2][:],
                                            func=AF.Copy, scale=invA_sb[:, ci:ci + 1]))
                    GP.wait_ge(s_bm[n % 2].h, BMv[n])
                    if k >= 4:
                        GP.wait_ge(s_in1.h, IN1_BASE0 + k - 3)
                    s_bx.inc(GP.tensor_tensor(out=bx_b[sl4][:], in0=xc[db][:],
                                              in1=bmrep[n % 2][:], op=ALU.mult))
                if k >= 1 and k - 1 < iters:
                    j = k - 1
                    n, db = j // dc_n, j % dc_n
                    sl = j % 2
                    sl4 = j % 4
                    sl6 = j % 6
                    ci = db * N + n
                    VE.wait_ge(s_ab.h, AB_BASE0 + j + 1)
                    VE.wait_ge(s_bx.h, BX_BASE0 + j + 1)
                    s_in1.inc(VE.scalar_tensor_tensor(out=in1_b[sl][:], in0=ab_b[sl6][:], scalar=1.0,
                                                      in1=bx_b[sl4][:], op0=ALU.subtract, op1=ALU.mult))
                    VE.wait_ge(s_in1.h, s_in1.n)
                    if j >= 2:
                        VE.wait_ge(s_yt.h, YT_BASE0 + j - 1)
                    if tc == 0:
                        init = 0.0
                    else:
                        VE.wait_ge(s_carry.h, carry_val[ci])
                        init = carry[:, ci:ci + 1]
                    s_scan.inc(VE.tensor_tensor_scan(out=sc_b[sl][:], data0=ab_b[sl6][:],
                                                     data1=in1_b[sl][:], initial=init,
                                                     op0=ALU.mult, op1=ALU.add))
                    VE.wait_ge(s_scan.h, s_scan.n)
                    if tc < tc_n - 1:
                        carry_val[ci] = s_carry.inc(
                            VE.tensor_copy(out=carry[:, ci:ci + 1], in_=sc_b[sl][:, TC - 1:TC]))
                    VE.wait_ge(s_icm.h, ICM_BASE0 + j + 1)
                    if j >= 4:
                        VE.wait_ge(s_yadd.h, YADD_BASE0 + j - 3)
                    s_yt.inc(VE.tensor_tensor(out=yt_b[sl4][:], in0=sc_b[sl][:],
                                              in1=icm_b[j % 2][:], op=ALU.mult))
                if k >= 2:
                    j = k - 2
                    n, db = j // dc_n, j % dc_n
                    sl4 = j % 4
                    GP.wait_ge(s_yt.h, YT_BASE0 + j + 1)
                    GP.wait_ge(s_yinit.h, YINIT_V)
                    if ylast[db] is not None:
                        GP.wait_ge(s_yadd.h, ylast[db])
                    ylast[db] = s_yadd.inc(GP.tensor_tensor(out=y_sb[db][:], in0=y_sb[db][:],
                                                            in1=yt_b[sl4][:], op=ALU.add))
                if k < iters:
                    n, db = k // dc_n, k % dc_n
                    if db == 0 and 1 <= n and n + 1 < N:
                        emit_repdma(n + 1)
                yield
            YADD_V = s_yadd.n

            PE.wait_ge(s_yadd.h, YADD_V)
            wait_tok(PE, stg_free["A"])
            last = None
            for db in range(dc_n):
                last = PE.matmul(out=stgA(0, 1, 0, TC), lhsT=onesc[:], rhs=y_sb[db][:],
                                 start=(db == 0), stop=(db == dc_n - 1))
            s_stgAr.inc(last)
            SC.wait_ge(s_stgAr.h, s_stgAr.n)
            ist = SC.activation(out=lnrow[0:1, 0:TC], in_=stgA(0, 1, 0, TC), func=AF.Copy, scale=1.0 / D)
            s_stgAf.inc(ist)
            stg_free["A"] = s_stgAf.tok()
            mean_tok = s_stgAf.tok()
            SQMM_BASE = s_sqmm.n
            wait_tok(PE, stg_free["A"])
            for db in range(dc_n):
                VE.wait_ge(s_yadd.h, YADD_V)
                if db >= 1:
                    VE.wait_ge(s_sqmm.h, SQMM_BASE + db)
                s_sq.inc(VE.tensor_tensor(out=sq_b[0][:], in0=y_sb[db][:],
                                          in1=y_sb[db][:], op=ALU.mult))
                PE.wait_ge(s_sq.h, s_sq.n)
                mm = PE.matmul(out=stgA(0, 1, 0, TC), lhsT=onescb[:], rhs=sq_b[0][:],
                               start=(db == 0), stop=(db == dc_n - 1))
                if db == dc_n - 1:
                    s_stgAr.inc(mm)
                else:
                    s_sqmm.inc(mm)
            SC.wait_ge(s_stgAr.h, s_stgAr.n)
            ist = SC.activation(out=lnrow[0:1, TC:2 * TC], in_=stgA(0, 1, 0, TC), func=AF.Copy, scale=1.0 / D)
            s_stgAf.inc(ist)
            stg_free["A"] = s_stgAf.tok()
            msq_tok = s_stgAf.tok()
            wait_tok(SY, mean_tok)
            wait_tok(SY, msq_tok)
            dma(SY, lnr_h.rearrange("r n -> (r n)")[None, :], lnrow[0:1, :], s_lnw)
            SY.wait_ge(s_lnw.h, s_lnw.n)
            SY.wait_ge(s_ab.h, s_ab.n)
            SY.wait_ge(s_x1n.h, s_x1n.n)
            dma(SY, drep[:], lnr_h[0:1, :].to_broadcast((128, TC)), s_bm[0])
            BM_V = s_bm[0].n
            dma(SY, rrep32[:], lnr_h[1:2, :].to_broadcast((128, TC)), s_cm[0])
            CM_V = s_cm[0].n
            VE.wait_ge(s_bm[0].h, BM_V)
            VE.wait_ge(s_cm[0].h, CM_V)
            s_vch.inc(VE.tensor_tensor(out=ln1tmp[:], in0=drep[:], in1=drep[:], op=ALU.mult))
            VE.wait_ge(s_vch.h, s_vch.n)
            s_lnr.inc(VE.tensor_tensor(out=rrep32[:], in0=rrep32[:], in1=ln1tmp[:], op=ALU.subtract))
            SC.wait_ge(s_lnr.h, s_lnr.n)
            s_sch.inc(SC.activation(out=rrep32[:], in_=rrep32[:], func=AF.Ln, bias=1e-5))
            SC.wait_ge(s_sch.h, s_sch.n)
            s_lnr.inc(SC.activation(out=rrep32[:], in_=rrep32[:], func=AF.Exp, scale=-0.5))
            LNR_V = s_lnr.n
            for db in range(dc_n):
                VE.wait_ge(s_lnr.h, LNR_V)
                s_vch.inc(VE.tensor_tensor(out=y_sb[db][:], in0=y_sb[db][:], in1=drep[:], op=ALU.subtract))
                VE.wait_ge(s_vch.h, s_vch.n)
                s_x1n.inc(VE.tensor_tensor(out=y_sb[db][:], in0=y_sb[db][:], in1=rrep32[:], op=ALU.mult))
            yield
            X1N_V = s_x1n.n
            for db in range(dc_n):
                SC.wait_ge(s_x1n.h, X1N_V)
                kq = s_x1.n
                if kq >= 1:
                    SC.wait_ge(s_x1d[0].h, s_x1d[0].n)
                s_x1.inc(SC.copy(out=x1stg[0][:], in_=y_sb[db][:]))
                SY.wait_ge(s_x1.h, s_x1.n)
                dma(SY, x1d_h[b, db, :, t0:t0 + TC], x1stg[0][:], s_x1d[0])
            for tb in range(tpc):
                PE.wait_ge(s_x1n.h, X1N_V)
                wait_tok(PE, stg_free["A"])
                last = None
                for dc in range(dc_n):
                    last = PE.matmul(out=stgA(0, 128, 0, E),
                                     lhsT=y_sb[dc][:, tb * 128:(tb + 1) * 128],
                                     rhs=gate_sb[:, dc * E:(dc + 1) * E],
                                     start=(dc == 0), stop=(dc == dc_n - 1))
                s_stgAr.inc(last)
                VE.wait_ge(s_stgAr.h, s_stgAr.n)
                s_gv.inc(VE.reduce_max(out=c_rowmax, in_=stgA(0, 128, 0, E), axis=AX.X))
                VE.wait_ge(s_gv.h, s_gv.n)
                s_gv.inc(VE.tensor_scalar(out=c_negmax, in0=c_rowmax, scalar1=-1.0,
                                          scalar2=None, op0=ALU.mult))
                SC.wait_ge(s_gv.h, s_gv.n)
                SC.wait_ge(s_g.h, s_g.n)
                ist = SC.activation(out=eg_sb[:], in_=stgA(0, 128, 0, E), func=AF.Exp, bias=c_negmax)
                s_gs.inc(ist)
                stg_free["A"] = s_gs.tok()
                VE.wait_ge(s_gs.h, s_gs.n)
                s_gv.inc(VE.max(out=m8_sb[:], in_=eg_sb[:]))
                VE.wait_ge(s_gv.h, s_gv.n)
                s_gv.inc(VE.tensor_scalar(out=mask_sb[:], in0=eg_sb[:], scalar1=m8_sb[:, 2:3],
                                          scalar2=None, op0=ALU.is_ge))
                VE.wait_ge(s_gv.h, s_gv.n)
                s_gv.inc(VE.tensor_tensor(out=gsel_sb[:], in0=eg_sb[:], in1=mask_sb[:], op=ALU.mult))
                VE.wait_ge(s_gv.h, s_gv.n)
                s_gv.inc(VE.reduce_sum(out=c_den, in_=gsel_sb[:], axis=AX.X))
                VE.wait_ge(s_gv.h, s_gv.n)
                s_gv.inc(VE.tensor_scalar(out=c_den2, in0=c_den, scalar1=1e-12,
                                          scalar2=None, op0=ALU.max))
                VE.wait_ge(s_gv.h, s_gv.n)
                s_gv.inc(VE.reciprocal(out=c_rec, in_=c_den2))
                VE.wait_ge(s_gv.h, s_gv.n)
                gc0 = (tc * tpc + tb) * E
                s_g.inc(VE.tensor_scalar(out=g_sb[gp_][:, gc0:gc0 + E], in0=gsel_sb[:],
                                         scalar1=c_rec, scalar2=None, op0=ALU.mult))
                PE.wait_ge(s_g.h, s_g.n)
                tcol0 = (tc * tpc + tb) * 128
                rep["gT"] = stage("A", E, 128,
                                  lambda gc0=gc0: PE.transpose(out=stgA(0, E, 0, 128),
                                                               in_=g_sb[gp_][:, gc0:gc0 + E],
                                                               identity=idn32[:]),
                                  SC, gT_sb[gp_][:, tcol0:tcol0 + 128], s_stgAf)
                yield
            gpe_token = rep["gT"]
        GT_TOK = rep["gT"]
        G_V = s_g.n
        X1D_V = (s_x1d[0].n, s_x1d[1].n)

        mstate[b] = dict(GT_TOK=GT_TOK, G_V=G_V, X1D_V=X1D_V)
        shared['gpe'] = gpe_token
        yield
    def ffn_emit(b, pump):
        gp_ = b % 2
        if b == 0:
            for ENG in (SY, SC):
                ENG.wait_ge(s_p0w[0].h, s_p0w[0].n)
                ENG.wait_ge(s_p0w[1].h, s_p0w[1].n)
                ENG.wait_ge(s_p0c.h, s_p0c.n)
                ENG.wait_ge(s_p0l.h, s_p0l.n)
                ENG.wait_ge(s_p0w2.h, s_p0w2.n)
        else:
            SC.wait_ge(s_cb.h, s_cb.n)
        GT_TOK = mstate[b]['GT_TOK']
        G_V = mstate[b]['G_V']
        X1D_V = mstate[b]['X1D_V']
        x1tg_free = ffn_state["x1tg_free"]
        ps2_free = ffn_state["ps2_free"]
        ogw_tok = ffn_state["ogw_tok"]
        pcol0 = gp_ * (tb_n * 3)
        HS_GUARD = s_hs.n
        rt0 = rt_sb[:, 0:E]; rt1 = rt_sb[:, E:2 * E]
        rt2 = rt_sb[:, 2 * E:3 * E]; rt3 = rt_sb[:, 3 * E:4 * E]
        rt4 = rt_sb[:, 4 * E:5 * E]
        pf_col = rt_sb[:, 5 * E:5 * E + 1]
        pf2_col = rt_sb[:, 5 * E + 1:5 * E + 2]
        pf3_col = rt_sb[:, 5 * E + 2:5 * E + 3]

        def load_x1tg(tg):
            slg_ = tg % 2
            SY.wait_ge(s_x1d[0].h, X1D_V[0])
            SY.wait_ge(s_x1d[1].h, X1D_V[1])
            wait_tok(SY, x1tg_free[slg_])
            for db in range(dc_n):
                dma(SY, x1tg[slg_][:, db * TC:(db + 1) * TC],
                    x1d_h[b, db, :, tg * TC:tg * TC + TC], s_x1l[slg_])
            x1l_v[slg_] = s_x1l[slg_].n

        x1l_v = [0, 0]
        accp = {}
        rnk_tok = {}
        PTR_V = {}
        load_x1tg(0)
        for tg in range(tg_n):
            slg = tg % 2
            X1L_V = x1l_v[slg]
            for ti in range(tpc):
                tile = tg * tpc + ti
                PE.wait_ge(s_x1l[slg].h, X1L_V)
                PE.wait_ge(s_psbf.h, s_psbf.n)
                last = None
                for dc in range(dc_n):
                    last = PE.transpose(out=PSB[:, dc * 128:(dc + 1) * 128],
                                        in_=x1tg[slg][:, dc * TC + ti * 128:dc * TC + (ti + 1) * 128],
                                        identity=idnb[:])
                s_psbr.inc(last)
                VE.wait_ge(s_psbr.h, s_psbr.n)
                s_psbf.inc(VE.tensor_copy(out=acc[:, tile * D:(tile + 1) * D], in_=PSB[:, 0:D]))
                accp[tile] = s_psbf.tok()
                wait_tok(VE, ffn_state.get("cnt_mm_tok"))
                s_rte.inc(VE.tensor_scalar(out=mskt_sb[:], in0=g_sb[gp_][:, tile * E:(tile + 1) * E],
                                           scalar1=1e-33, scalar2=None, op0=ALU.is_ge))
                MSK_V = s_rte.n

                def cnt_mm(MSK_V=MSK_V, first=(b > 0 and tile == 0)):
                    if first:
                        PE.wait_ge(s_hs.h, HS_GUARD)
                    PE.wait_ge(s_rte.h, MSK_V)
                    return PE.matmul(out=stgB(0, 1, 0, E), lhsT=onesc[:], rhs=mskt_sb[:],
                                     start=True, stop=True)
                stage("B", 1, E, cnt_mm, SC, cnt_sb[0:1, tile * E:(tile + 1) * E], s_stgBf)
                ffn_state["cnt_mm_tok"] = (s_stgBr, s_stgBr.n)
                pump(3)
            x1tg_free[slg] = (s_psbr, s_psbr.n)
            if tg + 1 < tg_n:
                load_x1tg(tg + 1)
        CNT_V = s_stgBf.n

        SY.wait_ge(s_stgBf.h, CNT_V)
        dma(SY, cntd_h.rearrange("a e -> (a e)")[None, :], cnt_sb[0:1, :], s_cb)
        SY.wait_ge(s_cb.h, s_cb.n)
        dma(SY, cntr_sb[0:tb_n, :], cntd_h, s_cb)
        CB_V = s_cb.n

        def base_mm():
            PE.wait_ge(s_cb.h, CB_V)
            return PE.matmul(out=stgA(0, tb_n, 0, E), lhsT=lt_sb[:], rhs=cntr_sb[:],
                             start=True, stop=True)
        BS_TOK = stage("A", tb_n, E, base_mm, SC, bs_sb[:], s_stgAf)
        wait_tok(SY, BS_TOK)
        dma(SY, based_h, bs_sb[:], s_cb)
        BD_V = s_cb.n

        bas_rd = ffn_state.setdefault("bas_rd", [None, None])
        rnk_rd = ffn_state.get("rnk_rd")
        srd_tok = ffn_state.get("srd_tok")
        for tile in range(tb_n):
            par = tile % 2
            c0 = pcol0 + tile * 3
            SY.wait_ge(s_cb.h, BD_V)
            wait_tok(SY, bas_rd[par])
            dma(SY, basebc[:, par * E:(par + 1) * E],
                based_h[tile:tile + 1, :].to_broadcast((128, E)), s_bas[par])
            gt_ = g_sb[gp_][:, tile * E:(tile + 1) * E]
            wait_tok(VE, rnk_rd)
            s_rte.inc(VE.tensor_scalar(out=rt4, in0=gt_, scalar1=1e-33, scalar2=None, op0=ALU.is_ge))
            MSK2_V = s_rte.n

            def rank_mm(MSK2_V=MSK2_V):
                PE.wait_ge(s_rte.h, MSK2_V)
                return PE.matmul(out=stgA(0, 128, 0, E), lhsT=ltri_sb[:], rhs=rt4,
                                 start=True, stop=True)
            wait_tok(SC, srd_tok)
            srk_tok = stage("A", 128, E, rank_mm, SC, srank_sb[:], s_stgAf)
            rnk_rd = (s_stgAr, s_stgAr.n)
            VE.wait_ge(s_bas[par].h, s_bas[par].n)
            wait_tok(VE, srk_tok)
            s_rte.inc(VE.tensor_tensor(out=rt0, in0=srank_sb[:],
                                       in1=basebc[:, par * E:(par + 1) * E], op=ALU.add))
            bas_rd[par] = s_rte.tok()
            srd_tok = s_rte.tok()
            s_rte.inc(VE.tensor_tensor_scan(out=rt1, data0=onesE_sb[:], data1=rt4,
                                            initial=0.0, op0=ALU.mult, op1=ALU.add))
            VE.wait_ge(s_rte.h, s_rte.n)
            s_rte.inc(VE.tensor_tensor(out=rt0, in0=rt0, in1=rt4, op=ALU.mult))
            for j in range(3):
                VE.wait_ge(s_rte.h, s_rte.n)
                s_rte.inc(VE.tensor_scalar(out=rt2, in0=rt1, scalar1=float(j + 1),
                                           scalar2=None, op0=ALU.is_equal))
                VE.wait_ge(s_rte.h, s_rte.n)
                s_rte.inc(VE.tensor_tensor(out=rt2, in0=rt2, in1=rt4, op=ALU.mult))
                VE.wait_ge(s_rte.h, s_rte.n)
                s_rte.inc(VE.tensor_tensor(out=rt3, in0=rt0, in1=rt2, op=ALU.mult))
                s_rte.inc(VE.reduce_sum(out=pf2_col, in_=rt2, axis=AX.X))
                VE.wait_ge(s_rte.h, s_rte.n)
                s_rte.inc(VE.reduce_sum(out=pf_col, in_=rt3, axis=AX.X))
                s_rte.inc(VE.tensor_scalar(out=pf3_col, in0=pf2_col, scalar1=1.0,
                                           scalar2=float(E * CAP + tile * 3 + j),
                                           op0=ALU.is_lt, op1=ALU.mult))
                VE.wait_ge(s_rte.h, s_rte.n)
                s_rte.inc(VE.tensor_tensor(out=pf_col, in0=pf_col, in1=pf3_col, op=ALU.add))
                VE.wait_ge(s_rte.h, s_rte.n)
                s_rte.inc(VE.tensor_copy(out=ptr_u[:, c0 + j:c0 + j + 1], in_=pf_col))
                s_rte.inc(VE.tensor_tensor(out=rt3, in0=gt_, in1=rt2, op=ALU.mult))
                VE.wait_ge(s_rte.h, s_rte.n)
                s_rte.inc(VE.reduce_sum(out=gval_sb[:, c0 + j:c0 + j + 1], in_=rt3, axis=AX.X))
            PTR_V[tile] = s_rte.n
            ffn_state["rnk_rd"] = rnk_rd
            ffn_state["srd_tok"] = srd_tok
            pump(3)
        GP.wait_ge(s_rte.h, PTR_V[tb_n - 1])
        wait_tok(GP, accp[tb_n - 1])
        GP.wait_ge(s_zf.h, ZF_ALL)
        if b > 0:
            wait_tok(GP, ffn_state["xgl_all"])
        for tile in range(tb_n):
            c0 = pcol0 + tile * 3
            for j in range(3):
                ist = GP.indirect_dma_start(
                    out=xg_h, out_offset=IndirectOffsetOnAxis(ptr_u[:, c0 + j:c0 + j + 1], 0),
                    in_=acc[:, tile * D:(tile + 1) * D], in_offset=None)
                s_xsc.inc(ist, 16)
        pump(6)
        XSC_ALL = s_xsc.n

        trn_tok = None
        w1ph_tok = None

        def load_xg(e_, dep_tok):
            SY.wait_ge(s_xsc.h, XSC_ALL)
            wait_tok(SY, dep_tok)
            for si in range(spc):
                dma(SY, x1tg[0][:, si * D:(si + 1) * D],
                    xg_h[e_ * CAP + si * 128:e_ * CAP + (si + 1) * 128, :], s_xgl)
            return s_xgl.n

        def t_round(si, first, xglv, war_tok):
            if first:
                wait_tok(PE, war_tok)
                PE.wait_ge(s_xgl.h, xglv)
            PE.wait_ge(s_psbf.h, s_psbf.n)
            last = None
            for dc in range(dc_n):
                last = PE.transpose(out=PSB[:, dc * 128:(dc + 1) * 128],
                                    in_=x1tg[0][:, si * D + dc * 128:si * D + (dc + 1) * 128],
                                    identity=idnb[:])
            s_psbr.inc(last)
            SC.wait_ge(s_psbr.h, s_psbr.n)
            if first:
                wait_tok(SC, war_tok)
            last = None
            for dc in range(dc_n):
                last = SC.copy(out=x1tg[1][:, dc * CAP + si * 128:dc * CAP + (si + 1) * 128],
                               in_=PSB[:, dc * 128:(dc + 1) * 128])
            s_psbf.inc(last)

        xgl_v = load_xg(0, x1tg_free[0])
        for si in range(spc):
            t_round(si, si == 0, xgl_v, x1tg_free[1])
        trn_tok = (s_psbr, s_psbr.n)
        TRNC_V = s_psbf.n
        if E > 1:
            xgl_v = load_xg(1, trn_tok)
        for g in range(hcg_n):
            w1_plan.append((0, g))
            w1_free_tok.append(None)
        for e in range(E):
            wcyc = b * E + e
            if e + 1 < E:
                for g in range(hcg_n):
                    w1_plan.append((e + 1, g))
                    w1_free_tok.append(None)
            pump_w1()
            if wcyc >= 1:
                wait_tok(SC, hs_free_tok[wcyc - 1])
            for g in range(hcg_n):
                dma(SC, w2sb[:, g * hcg * D:(g + 1) * hcg * D], w2s_h[e, g], s_w2)
            W2_V = s_w2.n
            pump(2)

            for hc in range(hc_n):
                g, hcl = hc // hcg, hc % hcg
                sl = hc % 2
                if hc == 0:
                    PE.wait_ge(s_psbf.h, TRNC_V)
                if sl == 1 and e == 0 and hc == 1:
                    wait_tok(PE, stg_free["B"])
                if hcl == 0:
                    wait_tok(PE, w1_load_tok[wcyc * hcg_n + g])
                k1 = len(ps1_free_at)
                if k1 >= 2:
                    PE.wait_ge(s_hs.h, ps1_free_at[k1 - 2])
                last = None
                for dc in range(dc_n):
                    o = hcl * dc_n * 128 + dc * 128
                    last = PE.matmul(out=ps1(sl, CAP),
                                     lhsT=w1buf[(wcyc * hcg_n + g) % 2][:, o:o + 128],
                                     rhs=x1tg[1][:, dc * CAP:dc * CAP + CAP],
                                     start=(dc == 0), stop=(dc == dc_n - 1))
                s_ps1r.inc(last)
                if hcl == hcg - 1:
                    w1_free_tok[wcyc * hcg_n + g] = (s_ps1r, s_ps1r.n)
                    pump_w1()
                SC.wait_ge(s_ps1r.h, s_ps1r.n)
                if hc == 0 and wcyc >= 1:
                    wait_tok(SC, hs_free_tok[wcyc - 1])
                if hc == 0 and wcyc == E and ffn_state["fadd_last"] is not None:
                    wait_tok(SC, ffn_state["fadd_last"])
                ist = SC.activation(out=hs[:, hc * CAP:(hc + 1) * CAP], in_=ps1(sl, CAP),
                                    func=AF.Relu, bias=b1_sb[:, hc * E + e:hc * E + e + 1])
                ps1_free_at.append(s_hs.inc(ist))
                if hc % 2 == 1:
                    pump(2)
            HS_V = s_hs.n
            w1ph_tok = (s_ps1r, s_ps1r.n)

            PE.wait_ge(s_w2.h, W2_V)
            for tb in range(spc):
                sl2 = tb % 2
                wait_tok(PE, ps2_free[sl2])
                last = None
                for hc in range(hc_n):
                    if tb == 0:
                        PE.wait_ge(s_hs.h, HS_V - hc_n + hc + 1)
                    for half in range(dhalf_n):
                        last = PE.matmul(out=PSF[:, sl2 * 1024 + half * 512:sl2 * 1024 + half * 512 + dhw],
                                         lhsT=hs[:, hc * CAP + tb * 128:hc * CAP + (tb + 1) * 128],
                                         rhs=w2sb[:, hc * D + half * 512:hc * D + half * 512 + dhw],
                                         start=(hc == 0), stop=(hc == hc_n - 1))
                s_ps2r.inc(last)
                if tb == spc - 1:
                    hs_free_tok.append((s_ps2r, s_ps2r.n))
                SC.wait_ge(s_ps2r.h, s_ps2r.n)
                wait_tok(SC, ogw_tok[0])
                ist = SC.copy(out=ogst[:], in_=PSF[:, sl2 * 1024:sl2 * 1024 + D])
                s_ogc.inc(ist)
                ps2_free[sl2] = s_ogc.tok()
                SY.wait_ge(s_ogc.h, s_ogc.n)
                dma(SY, og_h[e * CAP + tb * 128:e * CAP + (tb + 1) * 128, :],
                    ogst[:], s_ogw[0])
                ogw_tok[0] = s_ogw[0].tok()
                if e + 1 < E:
                    t_round(tb, tb == 0, xgl_v, w1ph_tok)
                pump(2)
            if e + 1 < E:
                trn_tok = (s_psbr, s_psbr.n)
                TRNC_V = s_psbf.n
                if e + 2 < E:
                    xgl_v = load_xg(e + 2, trn_tok)
        ffn_state["xgl_all"] = (s_xgl, s_xgl.n)
        x1tg_free[0] = trn_tok
        x1tg_free[1] = w1ph_tok

        OGW_V = (s_ogw[0].n, s_ogw[1].n)
        HSR_TOK = hs_free_tok[-1]
        fadd_tok = [None] * tb_n
        for tile in range(tb_n):
            par = tile % 2
            c0 = pcol0 + tile * 3
            at = acc[:, tile * D:(tile + 1) * D]
            if tile <= 1:
                GP.wait_ge(s_ogw[0].h, OGW_V[0])
                GP.wait_ge(s_ogw[1].h, OGW_V[1])
                GP.wait_ge(s_zf.h, ZF_ALL)
                wait_tok(GP, HSR_TOK)
            GP.wait_ge(s_rte.h, PTR_V[tile])
            if tile >= 2:
                wait_tok(GP, fadd_tok[tile - 2])
            for j in range(3):
                ist = GP.indirect_dma_start(
                    out=hs[:, (par * 3 + j) * D:(par * 3 + j + 1) * D], out_offset=None,
                    in_=og_h, in_offset=IndirectOffsetOnAxis(ptr_u[:, c0 + j:c0 + j + 1], 0))
                s_og3[par].inc(ist, 16)
            OG3_V = s_og3[par].n
            if tile == 0:
                wait_tok(PE, GT_TOK)
            wait_tok(PE, ps2_free[par])
            last = None
            for half in range(dhalf_n):
                last = PE.matmul(out=PSF[:, par * 1024 + half * 512:par * 1024 + half * 512 + dhw],
                                 lhsT=gT_sb[gp_][:, tile * 128:(tile + 1) * 128],
                                 rhs=b2_sb[:, half * 512:half * 512 + dhw],
                                 start=True, stop=True)
            s_ps2r.inc(last)
            B2_V = s_ps2r.n
            VE.wait_ge(s_ps2r.h, B2_V)
            if tile == 0:
                VE.wait_ge(s_xsc.h, XSC_ALL)
            s_vch.inc(VE.tensor_tensor(out=at, in0=at, in1=PSF[:, par * 1024:par * 1024 + D],
                                       op=ALU.add))
            ps2_free[par] = s_vch.tok()
            VE.wait_ge(s_og3[par].h, OG3_V)
            for j in range(3):
                VE.wait_ge(s_vch.h, s_vch.n)
                ist = VE.scalar_tensor_tensor(out=at, in0=hs[:, (par * 3 + j) * D:(par * 3 + j + 1) * D],
                                              scalar=gval_sb[:, c0 + j:c0 + j + 1], in1=at,
                                              op0=ALU.mult, op1=ALU.add)
                if j == 2:
                    s_accu.inc(ist)
                else:
                    s_vch.inc(ist)
            fadd_tok[tile] = (s_accu, s_accu.n)
            ln2t = hs[:, 6 * D:7 * D]
            VE.wait_ge(s_accu.h, s_accu.n)
            s_vch.inc(VE.reduce_sum(out=c_sum, in_=at, axis=AX.X))
            s_vch.inc(VE.tensor_tensor(out=ln2t, in0=at, in1=at, op=ALU.mult))
            VE.wait_ge(s_vch.h, s_vch.n)
            s_vch.inc(VE.reduce_sum(out=c_sq, in_=ln2t, axis=AX.X))
            VE.wait_ge(s_vch.h, s_vch.n)
            s_vch.inc(VE.tensor_scalar(out=c_mean, in0=c_sum, scalar1=1.0 / D, scalar2=None, op0=ALU.mult))
            s_vch.inc(VE.tensor_scalar(out=c_var, in0=c_sq, scalar1=1.0 / D, scalar2=None, op0=ALU.mult))
            VE.wait_ge(s_vch.h, s_vch.n)
            s_vch.inc(VE.tensor_tensor(out=c_m2, in0=c_mean, in1=c_mean, op=ALU.mult))
            VE.wait_ge(s_vch.h, s_vch.n)
            s_ln2v.inc(VE.tensor_tensor(out=c_var, in0=c_var, in1=c_m2, op=ALU.subtract))
            SC.wait_ge(s_ln2v.h, s_ln2v.n)
            s_sch.inc(SC.activation(out=c_lnv, in_=c_var, func=AF.Ln, bias=1e-5))
            SC.wait_ge(s_sch.h, s_sch.n)
            s_ln2s.inc(SC.activation(out=c_rstd, in_=c_lnv, func=AF.Exp, scale=-0.5))
            VE.wait_ge(s_ln2s.h, s_ln2s.n)
            if s_outv.n >= 1:
                VE.wait_ge(s_st[0].h, s_st[0].n)
            s_outv.inc(VE.tensor_scalar(out=out_sb[0][:], in0=at, scalar1=c_mean,
                                        scalar2=c_rstd, op0=ALU.subtract, op1=ALU.mult))
            SY.wait_ge(s_outv.h, s_outv.n)
            dma(SY, out_h[b, tile * 128:(tile + 1) * 128, :], out_sb[0][:], s_st[0])
            pump(2)
        ffn_state["fadd_last"] = (s_outv, s_outv.n)

    def drain(g):
        for _ in g:
            pass

    def pump_n(g, k):
        for _ in range(k):
            if next(g, 'DONE') == 'DONE':
                return False
        return True

    p0 = p0_gen()
    m0 = mamba_gen(0)
    alive_p, alive_m = True, True
    while alive_p or alive_m:
        if alive_m:
            alive_m = pump_n(m0, 12)
        if alive_p:
            alive_p = pump_n(p0, 1)
    P0W2_ALL = s_p0w2.n
    m1 = mamba_gen(1) if B > 1 else None
    def pump1(k=1):
        if m1 is not None:
            pump_n(m1, k)
    pump1(48)
    ffn_emit(0, pump1)
    if B > 1:
        drain(m1)
        ffn_emit(1, lambda k=1: None)

    SY.wait_ge(s_st[0].h, s_st[0].n)
    SY.wait_ge(s_st[1].h, s_st[1].n)
    ctx.close()
    return nc


def aux_inputs(cfg):
    D, E, tb_n, CAP = cfg.D, cfg.E, cfg.tb_n, cfg.CAP
    fc = np.zeros((128, 8), np.float32)
    fc[:, 0] = 1e-5
    fc[:, 1] = -0.5
    fc[:, 2] = np.float32(1.0) / np.float32(D)
    fc[:, 3] = np.float32(E * CAP)
    lt = np.triu(np.ones((tb_n + 1, tb_n), np.float32), 1)
    lt[tb_n, :] = 1.0
    return {
        "aux_fconst": fc,
        "aux_idn32": np.eye(128, dtype=np.float32),
        "aux_idnb": np.eye(128).astype(ml_dtypes.bfloat16),
        "aux_onesc": np.ones((128, 1), np.float32),
        "aux_onescb": np.ones((128, 1)).astype(ml_dtypes.bfloat16),
        "aux_ltri": np.triu(np.ones((128, 128), np.float32), 1),
        "aux_lt": lt,
        "aux_ecap": (np.arange(E, dtype=np.float32) * CAP).reshape(1, E),
        "aux_onesE": np.ones((128, E), np.float32),
        "aux_zrow": np.zeros((128, D), ml_dtypes.bfloat16),
    }


def make_in_map(cfg, inputs, xslice):
    m = {}
    m["L0_attn_sequence"] = np.ascontiguousarray(xslice)
    for k in ("L0_attn_A", "L0_attn_W_B", "L0_attn_W_C", "L0_attn_W_Delta",
              "L0_ffn_gate", "L0_ffn_W1", "L0_ffn_b1", "L0_ffn_W2", "L0_ffn_b2"):
        m[k] = np.ascontiguousarray(inputs[k])
    m["L0_attn_b_B"] = np.ascontiguousarray(inputs["L0_attn_b_B"]).reshape(cfg.N, 1)
    m["L0_attn_b_C"] = np.ascontiguousarray(inputs["L0_attn_b_C"]).reshape(cfg.N, 1)
    m["L0_attn_b_Delta"] = np.ascontiguousarray(inputs["L0_attn_b_Delta"]).reshape(1, 1)
    m["L0_attn_bias_Delta"] = np.ascontiguousarray(inputs["L0_attn_bias_Delta"]).reshape(1, 1)
    m.update(aux_inputs(cfg))
    return m


def kernel(**inputs):
    from concourse.bass_utils import run_bass_kernel_spmd
    cfg = CFG(B=2, T=2048, D=1024, H=2048)
    nc = bass.Bass("TRN2", target_bir_lowering=False)
    build(nc, cfg)
    x = inputs["L0_attn_sequence"]
    in_maps = [make_in_map(cfg, inputs, x[2 * c:2 * c + 2]) for c in range(8)]
    res = run_bass_kernel_spmd(nc, in_maps, core_ids=list(range(8)))
    return np.concatenate([r["out"] for r in res.results], axis=0)
